# version 14
# baseline (speedup 1.0000x reference)
import math
import numpy as np
import ml_dtypes
import concourse.bass as bass
import concourse.mybir as mybir
from concourse.bass_utils import run_bass_kernel_spmd

F32 = mybir.dt.float32
BF16 = mybir.dt.bfloat16
AF = mybir.ActivationFunctionType
ALU = mybir.AluOpType
AX = mybir.AxisListType

T = 2048
C = 2048
NB = 2
H = 8
HQ = 16
INTER = 8192
GATE_OFFSET = 6.906768
C2 = 1.0 / (64.0 * math.sqrt(2048.0))
CD = 1.0 / (64.0 * math.sqrt(8192.0))
BIAS_EP = -0.25 * math.log(128.0)
SBUF_TOP = 229344


class Sem:
    def __init__(self, nc, name):
        self.h = nc.semaphore(name).__enter__()
        self.n = 0

    def inc(self, instr, k):
        instr.then_inc(self.h, k)
        self.n += k
        return self.n


class B:
    def __init__(self, debug=False):
        self.debug = debug
        nc = bass.Bass("TRN2", target_bir_lowering=False)
        self.nc = nc
        self.wm = {}

        dt = nc.dram_tensor
        self.xin = dt("xin", [NB, T, C], F32, kind="ExternalInput").ap()
        self.wq = dt("wq", [C, C], F32, kind="ExternalInput").ap()
        self.wk = dt("wk", [C, 1024], F32, kind="ExternalInput").ap()
        self.wv = dt("wv", [C, 1024], F32, kind="ExternalInput").ap()
        self.wgi = dt("wgi", [C, H], F32, kind="ExternalInput").ap()
        self.wc = dt("wc", [C, C], F32, kind="ExternalInput").ap()
        self.bc = dt("bc", [1, C], F32, kind="ExternalInput").ap()
        self.su_ug = dt("su_ug", [1, C], F32, kind="ExternalInput").ap()
        self.su_dp = dt("su_dp", [1, INTER], F32, kind="ExternalInput").ap()
        self.wsug = dt("wsug", [1, 2 * INTER], F32, kind="ExternalInput").ap()
        self.wsdp = dt("wsdp", [1, C], F32, kind="ExternalInput").ap()
        self.wup = dt("wup", [INTER, C], F32, kind="ExternalInput").ap()
        self.wgate = dt("wgate", [INTER, C], F32, kind="ExternalInput").ap()
        self.wdown = dt("wdown", [C, INTER], F32, kind="ExternalInput").ap()
        self.h128 = dt("h128", [128, 128], F32, kind="ExternalInput").ap()
        self.tri = dt("tri", [128, 128], BF16, kind="ExternalInput").ap()
        self.eye8 = dt("eye8", [8, 8], F32, kind="ExternalInput").ap()
        self.onesrow = dt("onesrow", [1, 128], F32, kind="ExternalInput").ap()

        self.out = dt("out", [NB, T, C], F32, kind="ExternalOutput").ap()

        self.xbf = dt("xbf", [NB, T, C], BF16, kind="Internal").ap()
        self.wq_b = dt("wq_b", [C, C], BF16, kind="Internal").ap()
        self.wk_b = dt("wk_b", [C, 1024], BF16, kind="Internal").ap()
        self.wv_b = dt("wv_b", [C, 1024], BF16, kind="Internal").ap()
        self.wg_b = dt("wg_b", [C, H], BF16, kind="Internal").ap()
        self.wc_b = dt("wc_b", [C, C], BF16, kind="Internal").ap()
        self.wup_b = dt("wup_b", [INTER, C], BF16, kind="Internal").ap()
        self.wgate_b = dt("wgate_b", [INTER, C], BF16, kind="Internal").ap()
        self.wdown_b = dt("wdown_b", [C, INTER], BF16, kind="Internal").ap()
        self.wupT = dt("wupT", [C, INTER], BF16, kind="Internal").ap()
        self.wgateT = dt("wgateT", [C, INTER], BF16, kind="Internal").ap()
        self.wdT = dt("wdT", [INTER, C], BF16, kind="Internal").ap()
        xk = "ExternalOutput" if debug else "Internal"
        self.xln1 = dt("xln1", [NB, T, C], F32, kind=xk).ap()
        self.xln1b = dt("xln1b", [NB, T, C], BF16, kind="Internal").ap()
        if debug:
            self.d_qt = dt("d_qt", [NB, 128, HQ, T], BF16, kind="ExternalOutput").ap()
            self.d_kt = dt("d_kt", [NB, 128, H, T], BF16, kind="ExternalOutput").ap()
            self.d_v = dt("d_v", [NB, 128, 16, 1024], BF16, kind="ExternalOutput").ap()
            self.d_y = dt("d_y", [NB, 128, HQ, T], BF16, kind="ExternalOutput").ap()
            self.d_x2 = dt("d_x2", [128, 16, 512], BF16, kind="ExternalOutput").ap()
            self.d_x3 = dt("d_x3", [128, 64, 512], BF16, kind="ExternalOutput").ap()
            self.d_vec = dt("d_vec", [16, 128, 8], F32, kind="ExternalOutput").ap()
            self.d_s1c = dt("d_s1c", [16, 128, 8], F32, kind="ExternalOutput").ap()
            self.d_wsbc = dt("d_wsbc", [128, C], F32, kind="ExternalOutput").ap()

        base = (SBUF_TOP - nc.sbuf_bytes_remaining + 31) & ~31
        o = {
            "big0": base,
            "big12": base + 65536,
            "big3": base + 131072,
            "sl": base + 163840,
            "wring": base + 180224,
            "bcr": base + 192512,
            "misc": base + 200704,
        }
        self.o = o

        def at(name, shape, dtp, off):
            return nc.alloc_sbuf_tensor_at(name, shape, dtp, offset=off).ap()

        self.qt = at("qt", [128, HQ, T], BF16, o["big0"])
        self.kt = at("kt", [128, H, T], BF16, o["big12"])
        self.v = at("v", [128, 16, 1024], BF16, o["big12"] + 32768)
        self.wc_sb = at("wc_sb", [128, 16, C], BF16, o["big12"])
        self.xTq2 = [at("xTq", [128, 16, 512], BF16, o["big3"]),
                     at("xTqb", [128, 16, 512], BF16, o["big3"] + 16384)]
        self.xres = [at(f"xres{i}", [128, C], F32, o["big3"] + 16384 + 8192 * i) for i in range(2)]
        self.slab = [at(f"slab{i}", [128, 16, 256], BF16, o["sl"] + 8192 * i) for i in range(2)]
        self.a_sb = at("a_sb", [128, 8, 512], BF16, o["sl"])
        self.bfst = at("bfst", [128, C], BF16, o["sl"] + 8192)
        self.stg = [at(f"stg{i}", [128, 1024], BF16, o["wring"] + 2048 * i) for i in range(2)]
        self.epen = [at(f"epen{i}", [128, 512], F32, o["wring"] + 4096 + 2048 * i) for i in range(4)]
        self.rb_sb = at("rb_sb", [128, 512], F32, o["wring"] + 4096)
        self.nrm_t = at("nrm_t", [128, 512], F32, o["wring"] + 6144)
        self.x3 = at("x3", [128, 64, 512], BF16, o["big0"])
        self.uslab = [at(f"uslab{i}", [128, 16, 256], BF16, o["big12"] + 8192 * i) for i in range(2)]
        self.gslab = [at(f"gslab{i}", [128, 16, 256], BF16, o["big12"] + 16384 + 8192 * i) for i in range(2)]
        self.xlnT = at("xlnT", [128, 16, 512], BF16, o["big12"] + 32768)
        self.x2 = at("x2", [128, 16, 512], BF16, o["big12"] + 49152)
        self.xout = at("xout", [128, 4, C], F32, o["big3"])
        self.xlres = [at(f"xlres{i}", [128, 1024], F32, o["sl"] + 4096 * i) for i in range(2)]
        self.btmp = at("btmp", [128, 512], BF16, o["sl"] + 8192)
        self.ups = [at(f"ups{i}", [128, 512], BF16, o["sl"] + 9216 + 1024 * i) for i in range(2)]
        self.f6tmp = at("f6tmp", [128, 512], F32, o["sl"] + 11264)
        self.wdsl = [at(f"wdsl{i}", [128, 2048], BF16, o["wring"] + 4096 * i) for i in range(3)]
        self.bc_bc = at("bc_bc", [128, C], F32, o["bcr"])
        self.wsdp_bc = at("wsdp_bc", [128, C], F32, o["bcr"])
        m = o["misc"]
        def mk(name, shape, dtp, nbytes):
            nonlocal m
            t = at(name, shape, dtp, m)
            m += nbytes
            return t
        self.eye8_sb = mk("eye8_sb", [8, 8], F32, 32)
        self.h128f = mk("h128f", [128, 128], F32, 512)
        self.h128s = mk("h128s", [128, 128], BF16, 256)
        self.tri_sb = mk("tri_sb", [128, 128], BF16, 256)
        self.suug_c = mk("suug_c", [128, 16], F32, 64)
        self.sudp_c = mk("sudp_c", [128, 64], F32, 256)
        self.wsu_up = mk("wsu_up", [128, 64], F32, 256)
        self.wsu_gt = mk("wsu_gt", [128, 64], F32, 256)
        self.wg_sb = mk("wg_sb", [128, 16, 8], BF16, 256)
        self.ones_r = mk("ones_r", [1, 128], F32, 512)
        self.onescol = mk("onescol", [128, 1], BF16, 32)
        self.biases = mk("biases", [128, 8], F32, 32)
        self.gbuf0 = mk("gbuf0", [128, 512], F32, 2048)
        self.gbuf1 = mk("gbuf1", [128, 512], F32, 2048)
        self.r0row = mk("r0row", [1, 512], F32, 2048)
        self.r0row2 = mk("r0row2", [1, 512], F32, 2048)
        self.hs2 = [mk(f"hs2_{i}", [128, 128], BF16, 256) for i in range(2)]
        self.s1c = mk("s1c", [128, 32], F32, 128)
        self.vec = mk("vec", [128, 8], F32, 32)
        assert m <= SBUF_TOP, f"sbuf overflow: {m} > {SBUF_TOP}"
        self.misc_end = m

        self.pp = [nc.alloc_psum_tensor(f"pp{i}", [128, 1024], F32).ap() for i in range(4)]
        self.bank = [self.pp[i // 2][:, (i % 2) * 512:(i % 2 + 1) * 512] for i in range(8)]
        self.bank_guard = [None] * 8

        self.s_ld = Sem(nc, "s_ld")
        self.s_tp = Sem(nc, "s_tp")
        self.s_gp = Sem(nc, "s_gp")
        self.s_cast = Sem(nc, "s_cast")
        self.s_sc = Sem(nc, "s_sc")
        self.s_out = Sem(nc, "s_out")
        self.s_sl = [Sem(nc, "s_sl0"), Sem(nc, "s_sl1")]
        self.s_us = [Sem(nc, "s_us0"), Sem(nc, "s_us1")]
        self.s_gs = [Sem(nc, "s_gs0"), Sem(nc, "s_gs1")]
        self.s_wd = [Sem(nc, "s_wd0"), Sem(nc, "s_wd1"), Sem(nc, "s_wd2")]
        self.s_xq = [Sem(nc, "s_xq0"), Sem(nc, "s_xq1")]
        self.s_xr = [Sem(nc, "s_xr0"), Sem(nc, "s_xr1")]
        self.s_xl = [Sem(nc, "s_xl0"), Sem(nc, "s_xl1")]
        self.s_wc = Sem(nc, "s_wc")
        self.s_pe = Sem(nc, "s_pe")
        self.s_dve = Sem(nc, "s_dve")
        self.s_act = Sem(nc, "s_act")

        self.prep_q = []
        self.prep_i = 0
        self.cast_marks = {}
        self.slab_guard = [None, None]
        self.xTq_guard = [None, None]
        self.sc_mark = [0, 0]
        self.bfst_guard = 0

    def bcast_row(self, jr):
        return self.gbuf0[0:1, :]

    def W(self, eng, sem, val):
        if val is None or val <= 0:
            return
        key = (id(eng), id(sem))
        if self.wm.get(key, 0) >= val:
            return
        eng.wait_ge(sem.h, val)
        self.wm[key] = val

    def Wg(self, eng, guard):
        if guard is not None:
            self.W(eng, guard[0], guard[1])

    def dve_edge(self, instr):
        v = self.s_dve.inc(instr, 1)
        self.W(self.nc.vector, self.s_dve, v)
        return v

    def bank_wait(self, i):
        self.Wg(self.nc.tensor, self.bank_guard[i])

    def bank_done(self, i, sem, val):
        self.bank_guard[i] = (sem, val)

    def emit_init(self):
        nc = self.nc
        g, dv, te = nc.gpsimd, nc.vector, nc.tensor
        scast, sgp = self.s_cast, self.s_gp

        def cast(dst, src, name, parts=1):
            n = src.shape[0]
            step = n // parts
            for i in range(parts):
                g.dma_start(dst[i * step:(i + 1) * step], src[i * step:(i + 1) * step])\
                    .then_inc(scast.h, 16)
                scast.n += 16
            self.cast_marks[name] = scast.n
            g.wait_ge(scast.h, scast.n)

        cast(self.xbf[0], self.xin[0], "x0", parts=2)
        cast(self.wg_b, self.wgi, "wg")
        cast(self.wq_b, self.wq, "wq", parts=2)
        cast(self.wk_b, self.wk, "wk")
        cast(self.wv_b, self.wv, "wv")
        cast(self.xbf[1], self.xin[1], "x1", parts=2)
        cast(self.wc_b, self.wc, "wc", parts=2)
        cast(self.wup_b, self.wup, "wup", parts=4)
        cast(self.wgate_b, self.wgate, "wgate", parts=4)
        cast(self.wdown_b, self.wdown, "wdown", parts=4)

        def gload(dst, src, slow=False):
            g.dma_start(dst, src, allow_slow_non_contiguous=slow).then_inc(sgp.h, 16)
            sgp.n += 16

        gload(self.eye8_sb, self.eye8)
        gload(self.h128f, self.h128)
        gload(self.tri_sb, self.tri)
        gload(self.ones_r, self.onesrow)
        gload(self.suug_c, self.su_ug.rearrange("o (a p) -> (o p) a", p=128), slow=True)
        gload(self.sudp_c, self.su_dp.rearrange("o (a p) -> (o p) a", p=128), slow=True)
        gload(self.wsu_up, self.wsug[:, 0:INTER].rearrange("o (a p) -> (o p) a", p=128), slow=True)
        gload(self.wsu_gt, self.wsug[:, INTER:].rearrange("o (a p) -> (o p) a", p=128), slow=True)
        self.mark_gp_small = sgp.n
        g.wait_ge(sgp.h, sgp.n)
        self.W(g, scast, self.cast_marks["wg"])
        gload(self.wg_sb, self.wg_b.rearrange("(ck p) h -> p ck h", p=128), slow=True)
        self.mark_wg = sgp.n
        g.wait_ge(sgp.h, sgp.n)

        dv.memset(self.biases[:, 0:1], 1e-5)
        dv.memset(self.biases[:, 1:2], 1.0)
        dv.memset(self.biases[:, 2:3], -GATE_OFFSET)
        dv.memset(self.biases[:, 3:4], BIAS_EP)
        dv.memset(self.onescol, 1.0)
        dv.memset(self.gbuf1[32:40, 0:1], 0.0)
        self.W(dv, sgp, self.mark_gp_small)
        dv.tensor_scalar(self.wsu_up, self.wsu_up, 64.0, None, ALU.mult)
        dv.tensor_scalar(self.wsu_gt, self.wsu_gt, 64.0, None, ALU.mult)
        ii = dv.tensor_scalar(self.h128s, self.h128f, CD, None, ALU.mult)
        self.s_dve.inc(ii, 1)
        self.mark_dve_init = self.s_dve.n

        self.W(te, sgp, self.mark_gp_small)
        for jr in range(4):
            self.W(g, self.s_pe, self.s_pe.n)
            gload(self.bcast_row(jr), self.bc[:, jr * 512:(jr + 1) * 512])
            bk = 6 + (jr % 2)
            self.bank_wait(bk)
            self.W(te, sgp, sgp.n)
            mm = te.matmul(self.bank[bk], self.ones_r[0:1, :], self.bcast_row(jr),
                           start=True, stop=True)
            pv = self.s_pe.inc(mm, 1)
            self.W(dv, self.s_pe, pv)
            cp = dv.tensor_copy(self.bc_bc[:, jr * 512:(jr + 1) * 512], self.bank[bk])
            dvv = self.s_dve.inc(cp, 1)
            self.bank_done(bk, self.s_dve, dvv)
        self.mark_bcbc = self.s_dve.n

        for (nat, castm, dst, rows, cols) in [
            (self.wup_b, "wup", self.wupT, INTER, C),
            (self.wgate_b, "wgate", self.wgateT, INTER, C),
            (self.wdown_b, "wdown", self.wdT, C, INTER),
        ]:
            for cc in range(cols // 128):
                for rc in range(rows // 1024):
                    self.prep_q.append((nat, castm, dst, cc, rc))

    def prep_pump(self, k):
        sy = self.nc.sync
        n = 0
        while n < k and self.prep_i < len(self.prep_q):
            i = self.prep_i
            nat, castm, dst, cc, rc = self.prep_q[i]
            buf = self.stg[i % 2]
            self.W(sy, self.s_cast, self.cast_marks[castm])
            if i >= 2:
                self.W(sy, self.s_tp, 16 * (2 * (i - 2) + 2))
            ld = sy.dma_start(buf, nat[rc * 1024:(rc + 1) * 1024, cc * 128:(cc + 1) * 128],
                              transpose=True)
            self.s_tp.inc(ld, 16)
            self.W(sy, self.s_tp, 16 * (2 * i + 1))
            st = sy.dma_start(dst[cc * 128:(cc + 1) * 128, rc * 1024:(rc + 1) * 1024], buf)
            self.s_tp.inc(st, 16)
            self.prep_i += 1
            n += 1

    def prep_done(self):
        return 16 * 2 * len(self.prep_q)

    def emit_A(self, b):
        nc = self.nc
        te, dv, ac, sy, g = nc.tensor, nc.vector, nc.scalar, nc.sync, nc.gpsimd
        spe, sdve, sact, sld, sgp = self.s_pe, self.s_dve, self.s_act, self.s_ld, self.s_gp

        xbf_b = self.xbf[b]
        gq = self.gbuf0[0:8, :]
        e1q = self.gbuf0[32:40, :]
        tqb = [self.gbuf0[64:72, :], self.gbuf1[0:8, :]]
        off8 = self.gbuf1[32:40, 0:1]

        self.W(dv, sdve, self.mark_dve_init)
        if b > 0:
            dv.memset(self.gbuf1[32:40, 0:1], 0.0)

        slab_guard = self.slab_guard
        epen_guard = getattr(self, "epen_guard_state", [None] * 4)
        self.epen_guard_state = epen_guard
        ep_idx = getattr(self, "ep_idx_state", 0)
        wqr = self.wq_b.rearrange("(ck p) j -> p ck j", p=128)
        wkr = self.wk_b.rearrange("(ck p) j -> p ck j", p=128)
        wvr = self.wv_b.rearrange("(ck p) j -> p ck j", p=128)

        for tq in range(4):
            t0 = tq * 512
            xTq = self.xTq2[tq % 2]
            self.W(sy, self.s_cast, self.cast_marks[f"x{b}"])
            for gd in (self.xTq_guard[tq % 2] or []):
                self.Wg(sy, gd)
            sxq = self.s_xq[tq % 2]
            for gidx in range(16):
                ld = sy.dma_start(xTq[:, gidx, :],
                                  xbf_b[t0:t0 + 512, gidx * 128:(gidx + 1) * 128],
                                  transpose=True)
                sxq.inc(ld, 16)
            xT_ready = (sxq, sxq.n)

            self.bank_wait(4)
            self.W(te, xT_ready[0], xT_ready[1])
            self.W(te, sgp, self.mark_wg)
            for ck in range(16):
                mm = te.matmul(self.bank[4][0:8, :], self.wg_sb[:, ck, :], xTq[:, ck, :],
                               start=(ck == 0), stop=(ck == 15))
            gpe = spe.inc(mm, 1)
            self.W(dv, spe, gpe)
            cp = dv.tensor_copy(gq, self.bank[4][0:8, :])
            gdv = sdve.inc(cp, 1)
            self.bank_done(4, sdve, gdv)
            self.W(ac, sdve, gdv)
            a0 = ac.activation(e1q, gq, AF.Exp, bias=self.biases[0:8, 2:3], scale=-1.0)
            v0 = sact.inc(a0, 1)
            self.W(ac, sact, v0)
            ai = ac.activation(tqb[0], e1q, AF.Ln, bias=self.biases[0:8, 1:2], scale=1.0)
            av = sact.inc(ai, 1)
            self.W(dv, sact, av)
            cur = 0
            k = 1
            while k < 512:
                src, dst2 = tqb[cur], tqb[1 - cur]
                dv.tensor_tensor(dst2[:, k:512], src[:, k:512], src[:, 0:512 - k], ALU.add)
                self.dve_edge(dv.tensor_copy(dst2[:, 0:k], src[:, 0:k]))
                cur = 1 - cur
                k *= 2
            L1cq = tqb[cur]
            assert cur == 1, "L1cq must land at base partition 0 for the eye8 matmul"
            self.dve_edge(dv.tensor_scalar(L1cq, L1cq, off8, None, ALU.add))
            ii = dv.tensor_copy(off8, L1cq[:, 511:512])
            cum_dv = sdve.inc(ii, 1)
            self.W(dv, sdve, cum_dv)

            for s in range(4):
                sb = self.slab[s % 2]
                ssl = self.s_sl[s % 2]
                self.W(sy, self.s_cast, self.cast_marks["wv"])
                self.Wg(sy, slab_guard[s % 2])
                ld = sy.dma_start(sb, wvr[:, :, s * 256:(s + 1) * 256])
                ssl.inc(ld, 16)
                self.W(te, ssl, ssl.n)
                for tt in range(4):
                    bk = tt % 2
                    self.bank_wait(bk)
                    for ck in range(16):
                        mm = te.matmul(self.bank[bk][:, 0:256],
                                       xTq[:, ck, tt * 128:(tt + 1) * 128],
                                       sb[:, ck, :], start=(ck == 0), stop=(ck == 15))
                    pv = spe.inc(mm, 1)
                    if tt == 3:
                        slab_guard[s % 2] = (spe, pv)
                    self.W(dv, spe, pv)
                    ev = dv.tensor_copy(self.v[:, tq * 4 + tt, s * 256:(s + 1) * 256],
                                        self.bank[bk][:, 0:256])
                    dvv = sdve.inc(ev, 1)
                    self.bank_done(bk, sdve, dvv)

            self.W(te, sdve, cum_dv)
            r0 = [self.r0row, self.r0row2]
            r0_guard = getattr(self, "r0_guard", [None, None])
            self.r0_guard = r0_guard
            self.bank_wait(5)
            mm = te.matmul(self.bank[5][0:1, :], self.eye8_sb[:, 0:1], L1cq,
                           start=True, stop=True)
            pv = spe.inc(mm, 1)
            self.W(dv, spe, pv)
            self.Wg(dv, r0_guard[0])
            cpy = dv.tensor_copy(r0[0], self.bank[5][0:1, :])
            dvv = sdve.inc(cpy, 1)
            self.bank_done(5, sdve, dvv)
            copy_dv = dvv
            for h in range(H):
                self.bank_wait(6)
                self.W(te, sdve, copy_dv)
                mm = te.matmul(self.bank[6], self.ones_r[0:1, :], r0[h % 2],
                               start=True, stop=True)
                pv = spe.inc(mm, 1)
                r0_guard[h % 2] = (spe, pv)
                iep = ep_idx % 4
                ien = (ep_idx + 1) % 4
                ep_idx += 2
                self.W(ac, spe, pv)
                self.Wg(ac, epen_guard[iep])
                ac.activation(self.epen[iep], self.bank[6], AF.Exp,
                              bias=self.biases[:, 3:4], scale=-0.5)
                self.Wg(ac, epen_guard[ien])
                ai = ac.activation(self.epen[ien], self.bank[6], AF.Exp,
                                   bias=self.biases[:, 3:4], scale=0.5)
                av = sact.inc(ai, 1)
                self.bank_done(6, sact, av)
                if h + 1 < H:
                    self.bank_wait(5)
                    mm = te.matmul(self.bank[5][0:1, :], self.eye8_sb[:, h + 1:h + 2],
                                   L1cq, start=True, stop=True)
                    pv = spe.inc(mm, 1)
                    self.W(dv, spe, pv)
                    self.Wg(dv, r0_guard[(h + 1) % 2])
                    cpy = dv.tensor_copy(r0[(h + 1) % 2], self.bank[5][0:1, :])
                    copy_dv = sdve.inc(cpy, 1)
                    self.bank_done(5, sdve, copy_dv)

                sb = self.slab[h % 2]
                ssl = self.s_sl[h % 2]
                self.W(sy, self.s_cast, self.cast_marks["wq"])
                self.Wg(sy, slab_guard[h % 2])
                ld = sy.dma_start(sb, wqr[:, :, h * 256:(h + 1) * 256])
                ssl.inc(ld, 16)
                self.W(te, ssl, ssl.n)
                for jl in range(2):
                    jt = 2 * h + jl
                    bk = jl
                    self.bank_wait(bk)
                    for ck in range(16):
                        mm = te.matmul(self.bank[bk], sb[:, ck, jl * 128:(jl + 1) * 128],
                                       xTq[:, ck, :], start=(ck == 0), stop=(ck == 15))
                    pv = spe.inc(mm, 1)
                    if jl == 1:
                        slab_guard[h % 2] = (spe, pv)
                    self.W(dv, spe, pv)
                    self.W(dv, sact, av)
                    ev = dv.tensor_tensor(self.qt[:, jt, t0:t0 + 512], self.bank[bk],
                                          self.epen[iep], ALU.mult)
                    dvv = sdve.inc(ev, 1)
                    self.bank_done(bk, sdve, dvv)
                sb2 = self.slab[(h + 1) % 2]
                ssl2 = self.s_sl[(h + 1) % 2]
                self.W(sy, self.s_cast, self.cast_marks["wk"])
                self.Wg(sy, slab_guard[(h + 1) % 2])
                ld = sy.dma_start(sb2[:, :, 0:128], wkr[:, :, h * 128:(h + 1) * 128])
                ssl2.inc(ld, 16)
                self.W(te, ssl2, ssl2.n)
                self.bank_wait(2)
                for ck in range(16):
                    mm = te.matmul(self.bank[2], sb2[:, ck, 0:128], xTq[:, ck, :],
                                   start=(ck == 0), stop=(ck == 15))
                pv = spe.inc(mm, 1)
                slab_guard[(h + 1) % 2] = (spe, pv)
                self.W(dv, spe, pv)
                ev = dv.tensor_tensor(self.kt[:, h, t0:t0 + 512], self.bank[2],
                                      self.epen[ien], ALU.mult)
                dvv = sdve.inc(ev, 1)
                self.bank_done(2, sdve, dvv)
                epen_guard[iep] = (sdve, sdve.n)
                epen_guard[ien] = (sdve, dvv)
            self.xTq_guard[tq % 2] = [(spe, spe.n)]
            self.prep_pump(6)

        self.ep_idx_state = ep_idx
        proj_dve = sdve.n
        if self.debug:
            self.W(sy, sdve, proj_dve)
            for dst, src in [(self.d_qt[b], self.qt), (self.d_kt[b], self.kt),
                             (self.d_v[b], self.v)]:
                st = sy.dma_start(dst, src)
                self.s_out.inc(st, 16)
            dbg_mark = self.s_out.n

        self.W(te, sdve, proj_dve)
        a_guard = [None] * 8
        ag0 = slab_guard[0][1] if slab_guard[0] else 0
        ag1 = slab_guard[1][1] if slab_guard[1] else 0
        for _i in range(4):
            a_guard[_i] = (spe, ag0)
            a_guard[4 + _i] = (spe, ag1)
        self.bfst_slab1_guard = (spe, ag1)
        qk_pair = 0
        first_y = True
        pending = []

        def fin_group(p):
            nonlocal first_y
            (phq, ptcol0, pyb, pnb, pypv) = p
            tmp = self.nrm_t[0:1, :]
            rec = self.rb_sb[0:1, :]
            self.W(dv, spe, pypv)
            self.dve_edge(dv.tensor_scalar(tmp, self.bank[pnb][0:1, :], 1e-6, None, ALU.add))
            ri = dv.reciprocal(rec, tmp)
            rdv = sdve.inc(ri, 1)
            self.W(te, sdve, rdv)
            mm = te.matmul(self.bank[pnb], self.ones_r[0:1, :], rec, start=True, stop=True)
            pv = spe.inc(mm, 1)
            self.W(dv, spe, pv)
            cpy = dv.tensor_copy(self.rb_sb, self.bank[pnb])
            dvv = self.dve_edge(cpy)
            self.bank_done(pnb, sdve, dvv)
            if first_y and self.debug:
                self.W(dv, self.s_out, dbg_mark)
            first_y = False
            ev = dv.tensor_tensor(self.qt[:, phq, ptcol0:ptcol0 + 512], self.bank[pyb],
                                  self.rb_sb, ALU.mult)
            dvv = sdve.inc(ev, 1)
            self.bank_done(pyb, sdve, dvv)

        class Grp:
            def __init__(g, hq, tr, sub):
                g.hq = hq
                g.h = hq // 2
                g.tr = tr
                g.tcol0 = tr * 512
                g.S = 4 * tr + 4
                g.sub = sub
                g.yb = 4 + sub
                g.nb = 6 + sub
                g.st = 0
                g.chunk_par = 0
                g.pend = None

        def emit_qk(g):
            fused = (g.st + 1 < 4 * g.tr)
            nst = 2 if fused else 1
            pq = qk_pair[0] % 2
            qk_pair[0] += 1
            pairp = self.pp[pq]
            los = []
            for u in range(nst):
                sti = g.st + u
                kk = sti - 4 * g.tr
                lo = 0 if kk < 0 else 128 * kk
                los.append(lo)
                self.bank_wait(2 * pq + u)
                mm = te.matmul(pairp[:, u * 512 + lo:(u + 1) * 512],
                               self.kt[:, g.h, sti * 128:(sti + 1) * 128],
                               self.qt[:, g.hq, g.tcol0 + lo:g.tcol0 + 512],
                               start=True, stop=True)
            qkv_ = spe.inc(mm, 1)
            slot = g.sub * 4 + g.chunk_par * 2
            g.chunk_par ^= 1
            self.W(ac, spe, qkv_)
            for u in range(nst):
                self.Wg(ac, a_guard[slot + u])
            if fused:
                ai = ac.activation(
                    self.a_sb[:, slot:slot + 2, :].rearrange("p a b -> p (a b)"),
                    pairp, AF.Square)
            else:
                lo = los[0]
                ai = ac.activation(self.a_sb[:, slot, lo:512], pairp[:, lo:512], AF.Square)
            av = sact.inc(ai, 1)
            for u in range(nst):
                self.bank_done(2 * pq + u, sact, av)
            if fused:
                g.pend = (slot, nst, los, g.st, "act", av)
            else:
                lo = los[0]
                self.W(dv, sact, av)
                mi = dv.tensor_tensor(self.a_sb[:, slot, lo:lo + 128],
                                      self.a_sb[:, slot, lo:lo + 128],
                                      self.tri_sb, ALU.mult)
                dvv = sdve.inc(mi, 1)
                g.pend = (slot, nst, los, g.st, "dve", dvv)
            g.st += nst

        def emit_yn(g):
            (slot, nst, los, st0, wk, wv) = g.pend
            g.pend = None
            if wk == "act":
                self.W(te, sact, wv)
            else:
                self.W(te, sdve, wv)
            for u in range(nst):
                sti = st0 + u
                lo = los[u]
                te.matmul(self.bank[g.yb][:, lo:512],
                          self.v[:, sti, g.h * 128:(g.h + 1) * 128],
                          self.a_sb[:, slot + u, lo:512],
                          start=(sti == 0), stop=(sti == g.S - 1))
                mm2 = te.matmul(self.bank[g.nb][0:1, lo:512], self.onescol,
                                self.a_sb[:, slot + u, lo:512],
                                start=(sti == 0), stop=(sti == g.S - 1))
                pv = spe.inc(mm2, 1)
                a_guard[slot + u] = (spe, pv)

        qk_pair = [0]
        pairs = [(hq, tr) for hq in range(HQ) for tr in range(4)]
        for gi in range(0, len(pairs), 2):
            g0 = Grp(*pairs[gi], 0)
            g1 = Grp(*pairs[gi + 1], 1)
            for p in pending:
                fin_group(p)
            pending = []
            self.bank_wait(g0.yb)
            self.bank_wait(g0.nb)
            self.bank_wait(g1.yb)
            self.bank_wait(g1.nb)
            ready = {0: None, 1: None}
            while True:
                live = False
                for gg in (g0, g1):
                    if gg.st < gg.S:
                        live = True
                        emit_qk(gg)
                    if ready[gg.sub] is not None:
                        saved, gg.pend = gg.pend, ready[gg.sub]
                        emit_yn(gg)
                        gg.pend = saved
                    ready[gg.sub] = gg.pend
                    gg.pend = None
                if not live and ready[0] is None and ready[1] is None:
                    break
            pending = [(g0.hq, g0.tcol0, g0.yb, g0.nb, spe.n),
                       (g1.hq, g1.tcol0, g1.yb, g1.nb, spe.n)]
            self.prep_pump(3)
        for p in pending:
            fin_group(p)
        pending = []
        self.bfst_slab1_guard = (spe, spe.n)
        y_dve = sdve.n
        if self.debug:
            self.W(sy, sdve, y_dve)
            st = sy.dma_start(self.d_y[b], self.qt)
            self.s_out.inc(st, 16)

        self.W(sy, spe, spe.n)
        self.W(sy, self.s_cast, self.cast_marks["wc"])
        wcr = self.wc_b.rearrange("(ck p) j -> p ck j", p=128)
        for part in range(4):
            ld = sy.dma_start(self.wc_sb[:, 4 * part:4 * part + 4, :],
                              wcr[:, 4 * part:4 * part + 4, :])
            self.s_wc.inc(ld, 16)
        wc_ready = self.s_wc.n

        xres_guard = [[(spe, spe.n)], [(spe, spe.n)]]

        def fin_tile(tt):
            xr = self.xres[tt % 2]
            gcol = (tt % 2) * 8
            s1 = self.s1c[:, gcol:gcol + 8]
            vec = self.vec
            self.dve_edge(dv.scalar_tensor_tensor(vec[:, 5:6], s1[:, 0:1], 1.0, s1[:, 1:2],
                                                  ALU.bypass, ALU.add))
            self.dve_edge(dv.scalar_tensor_tensor(vec[:, 6:7], s1[:, 2:3], 1.0, s1[:, 3:4],
                                                  ALU.bypass, ALU.add))
            self.dve_edge(dv.tensor_tensor(vec[:, 0:1], vec[:, 5:6], vec[:, 6:7], ALU.add))
            self.dve_edge(dv.scalar_tensor_tensor(vec[:, 5:6], s1[:, 4:5], 1.0, s1[:, 5:6],
                                                  ALU.bypass, ALU.add))
            self.dve_edge(dv.scalar_tensor_tensor(vec[:, 6:7], s1[:, 6:7], 1.0, s1[:, 7:8],
                                                  ALU.bypass, ALU.add))
            self.dve_edge(dv.tensor_tensor(vec[:, 1:2], vec[:, 5:6], vec[:, 6:7], ALU.add))
            self.dve_edge(dv.tensor_scalar(vec[:, 0:1], vec[:, 0:1], 1.0 / 2048.0, None,
                                           ALU.mult))
            self.dve_edge(dv.tensor_scalar(vec[:, 1:2], vec[:, 1:2], 1.0 / 2048.0, None,
                                           ALU.mult))
            self.dve_edge(dv.scalar_tensor_tensor(vec[:, 2:3], vec[:, 0:1], 1.0, vec[:, 0:1],
                                                  ALU.bypass, ALU.mult))
            si = dv.tensor_tensor(vec[:, 1:2], vec[:, 1:2], vec[:, 2:3], ALU.subtract)
            sv = sdve.inc(si, 1)
            self.W(ac, sdve, sv)
            a0 = ac.activation(vec[:, 3:4], vec[:, 1:2], AF.Ln, bias=self.biases[:, 0:1],
                               scale=1.0)
            v0 = sact.inc(a0, 1)
            self.W(ac, sact, v0)
            ai = ac.activation(vec[:, 3:4], vec[:, 3:4], AF.Exp, scale=-0.5)
            av = sact.inc(ai, 1)
            self.W(dv, sact, av)
            ni = dv.tensor_scalar(xr, xr, vec[:, 0:1], vec[:, 3:4],
                                  ALU.subtract, ALU.mult)
            nv = sdve.inc(ni, 1)
            if self.debug and b == 0:
                self.W(sy, sdve, nv)
                dst = sy.dma_start(self.d_vec[tt], self.vec)
                self.s_out.inc(dst, 16)
                dst = sy.dma_start(self.d_s1c[tt], self.s1c[:, gcol:gcol + 8])
                sov = self.s_out.inc(dst, 16)
                self.W(dv, self.s_out, sov)
            self.W(g, sdve, nv)
            st1 = g.dma_start(self.xln1[b, tt * 128:(tt + 1) * 128, :], xr)
            st1v = self.s_sc.inc(st1, 16)
            self.W(dv, self.s_sc, self.bfst_guard)
            self.Wg(dv, self.bfst_slab1_guard)
            self.bfst_slab1_guard = None
            ci = dv.tensor_copy(self.bfst, xr)
            cv = sdve.inc(ci, 1)
            self.W(g, sdve, cv)
            st2 = g.dma_start(self.xln1b[b, tt * 128:(tt + 1) * 128, :], self.bfst)
            self.s_sc.inc(st2, 16)
            self.bfst_guard = self.s_sc.n
            xres_guard[tt % 2] = [(self.s_sc, st1v), (sdve, cv)]

        for tt in range(16):
            xr = self.xres[tt % 2]
            gcol = (tt % 2) * 8
            for gd in (xres_guard[tt % 2] or []):
                self.Wg(g, gd)
            ldr = g.dma_start(xr, self.xin[b, tt * 128:(tt + 1) * 128, :])
            sxr = self.s_xr[tt % 2]
            xr_ready = sxr.inc(ldr, 16)
            self.W(dv, sxr, xr_ready)
            self.W(dv, sdve, self.mark_bcbc)
            dv.memset(self.s1c[:, gcol:gcol + 8], 0.0)
            dv.tensor_tensor(xr, xr, self.bc_bc, ALU.add)
            for jr in range(4):
                bk = jr
                self.bank_wait(bk)
                self.W(te, self.s_wc, wc_ready)
                self.W(te, sdve, y_dve)
                for ck in range(16):
                    mm = te.matmul(self.bank[bk], self.qt[:, ck, tt * 128:(tt + 1) * 128],
                                   self.wc_sb[:, ck, jr * 512:(jr + 1) * 512],
                                   start=(ck == 0), stop=(ck == 15))
                pv = spe.inc(mm, 1)
                self.W(dv, spe, pv)
                ev = dv.scalar_tensor_tensor(xr[:, jr * 512:(jr + 1) * 512], self.bank[bk],
                                             1.0, xr[:, jr * 512:(jr + 1) * 512],
                                             ALU.bypass, ALU.add,
                                             accum_out=self.s1c[:, gcol + jr:gcol + jr + 1])
                dvv = sdve.inc(ev, 1)
                self.bank_done(bk, sdve, dvv)
            for hh in range(4):
                dv.scalar_tensor_tensor(self.epen[hh % 2], xr[:, hh * 512:(hh + 1) * 512],
                                        1.0, xr[:, hh * 512:(hh + 1) * 512],
                                        ALU.bypass, ALU.mult,
                                        accum_out=self.s1c[:, gcol + 4 + hh:gcol + 5 + hh])
            if tt >= 1:
                fin_tile(tt - 1)
            self.prep_pump(2)
        fin_tile(15)
        self.slab_guard[0] = (self.s_sc, self.s_sc.n)
        self.slab_guard[1] = (self.s_sc, self.s_sc.n)
        g4 = [(self.s_sc, self.s_sc.n), (self.s_dve, self.s_dve.n)]
        self.xTq_guard[0] = g4
        self.xTq_guard[1] = g4
        self.a4_done = {"pe": spe.n, "dve": sdve.n, "sc": self.s_sc.n, "act": sact.n}
        self.sc_mark[b] = self.s_sc.n

    def emit_F(self):
        nc = self.nc
        te, dv, ac, sy, g = nc.tensor, nc.vector, nc.scalar, nc.sync, nc.gpsimd
        spe, sdve, sact, sld, sgp = self.s_pe, self.s_dve, self.s_act, self.s_ld, self.s_gp

        self.prep_pump(10 ** 9)

        self.W(te, sdve, self.a4_done["dve"])
        for jr in range(4):
            self.W(g, spe, spe.n if jr > 0 else self.a4_done["pe"])
            g.dma_start(self.bcast_row(jr), self.wsdp[:, jr * 512:(jr + 1) * 512])\
                .then_inc(sgp.h, 16)
            sgp.n += 16
            bk = 6 + (jr % 2)
            self.bank_wait(bk)
            self.W(te, sgp, sgp.n)
            mm = te.matmul(self.bank[bk], self.ones_r[0:1, :], self.bcast_row(jr),
                           start=True, stop=True)
            pv = spe.inc(mm, 1)
            self.W(dv, spe, pv)
            cp = dv.tensor_scalar(self.wsdp_bc[:, jr * 512:(jr + 1) * 512], self.bank[bk],
                                  64.0, None, ALU.mult)
            dvv = sdve.inc(cp, 1)
            self.bank_done(bk, sdve, dvv)
        wsdp_ready = sdve.n
        if self.debug:
            self.W(sy, sdve, wsdp_ready)
            dst = sy.dma_start(self.d_wsbc, self.wsdp_bc)
            self.s_out.inc(dst, 16)

        wupr = self.wupT.rearrange("(ck p) i -> p ck i", p=128)
        wgr = self.wgateT.rearrange("(ck p) i -> p ck i", p=128)

        a4pe = self.a4_done["pe"]
        uslab_guard = [(spe, a4pe), (spe, a4pe)]
        gslab_guard = [(spe, a4pe), (spe, a4pe)]
        wd_guard = [(sdve, self.a4_done["dve"])] * 3
        xlres_guard = [None, None]
        hs2_guard = [None, None]
        ups_guard = [None, None]
        xout_guard = [(self.s_sc, self.a4_done["sc"])]
        x2_guard = None
        x3_guard = None
        btmp_first = True

        for c in range(8):
            b = c // 4
            trange = (c % 4) * 512
            if c == 0:
                self.W(sy, self.s_sc, self.sc_mark[b])
                self.W(sy, spe, a4pe)
                for gidx in range(16):
                    ld = sy.dma_start(self.xlnT[:, gidx, :],
                                      self.xln1b[b, trange:trange + 512,
                                                 gidx * 128:(gidx + 1) * 128],
                                      transpose=True)
                    sld.inc(ld, 16)
                self.xlnT_ready = sld.n
            xlnT_ready = self.xlnT_ready

            for gidx in range(16):
                hsb = self.hs2[gidx % 2]
                if c == 0 or gidx >= 2:
                    self.Wg(dv, hs2_guard[gidx % 2])
                    hi = dv.tensor_scalar(hsb, self.h128f, self.suug_c[:, gidx:gidx + 1],
                                          C2, ALU.mult, ALU.mult)
                    hv = sdve.inc(hi, 1)
                else:
                    hv = self.hs2_pre[gidx]
                bk = 4 + (gidx % 2)
                self.bank_wait(bk)
                self.W(te, sdve, hv)
                self.W(te, sld, xlnT_ready)
                if x2_guard is not None:
                    self.Wg(te, x2_guard)
                mm = te.matmul(self.bank[bk], hsb, self.xlnT[:, gidx, :], start=True, stop=True)
                pv = spe.inc(mm, 1)
                hs2_guard[gidx % 2] = (spe, pv)
                self.W(dv, spe, pv)
                ev = dv.tensor_copy(self.x2[:, gidx, :], self.bank[bk])
                dvv = sdve.inc(ev, 1)
                self.bank_done(bk, sdve, dvv)
            if btmp_first:
                self.W(dv, self.s_sc, self.a4_done["sc"])
                btmp_first = False
            kk = 1
            while kk < 16:
                r = 0
                while r < 16:
                    for j in range(kk):
                        a1 = self.x2[:, r + j, :]
                        b1 = self.x2[:, r + j + kk, :]
                        dv.tensor_copy(self.btmp, a1)
                        dv.tensor_tensor(a1, self.btmp, b1, ALU.add)
                        dv.tensor_tensor(b1, self.btmp, b1, ALU.subtract)
                    r += 2 * kk
                kk *= 2
            ii = dv.engine_nop()
            x2_dv = sdve.inc(ii, 1)
            self.f2_pe_done = spe.n

            bf_done = [0] * 64
            bf_prod = [False] * 64
            bf_comp = [None] * 64
            bf_order = []
            bf_last = [-10] * 64
            bf_ctr = [0]
            BF_DIST = 6

            def bf_pump(final=False):
                while True:
                    progress = True
                    stuck = False
                    while progress:
                        progress = False
                        for s in range(6):
                            step = 1 << s
                            for i0 in range(0, 64, 2 * step):
                                for j in range(step):
                                    i = i0 + j
                                    jj = i + step
                                    if not (bf_prod[i] and bf_prod[jj]
                                            and bf_done[i] == s and bf_done[jj] == s):
                                        continue
                                    if max(bf_last[i], bf_last[jj]) > bf_ctr[0] - BF_DIST:
                                        stuck = True
                                        continue
                                    a1 = self.x3[:, i, :]
                                    b1 = self.x3[:, jj, :]
                                    dv.tensor_copy(self.btmp, a1)
                                    dv.tensor_tensor(a1, self.btmp, b1, ALU.add)
                                    ii2 = dv.tensor_tensor(b1, self.btmp, b1, ALU.subtract)
                                    bf_ctr[0] += 3
                                    bf_done[i] += 1
                                    bf_done[jj] += 1
                                    bf_last[i] = bf_last[jj] = bf_ctr[0]
                                    progress = True
                                    if final:
                                        bf_last_instr[0] = ii2
                                    if bf_done[i] == 6:
                                        mk = sdve.inc(ii2, 1)
                                        bf_comp[i] = mk
                                        bf_comp[jj] = mk
                                        bf_order.append(i)
                                        bf_order.append(jj)
                    if final and stuck:
                        self.dve_edge(bf_last_instr[0])
                        for i in range(64):
                            bf_last[i] = -10
                        continue
                    return

            bf_last_instr = [None]

            for ig in range(32):
                us = self.uslab[ig % 2]
                gs = self.gslab[ig % 2]
                sus = self.s_us[ig % 2]
                sgs = self.s_gs[ig % 2]
                self.W(sy, self.s_tp, self.prep_done())
                self.Wg(sy, uslab_guard[ig % 2])
                ld = sy.dma_start(us, wupr[:, :, ig * 256:(ig + 1) * 256])
                sus.inc(ld, 16)
                self.Wg(sy, gslab_guard[ig % 2])
                ld = sy.dma_start(gs, wgr[:, :, ig * 256:(ig + 1) * 256])
                sgs.inc(ld, 16)
                for il in range(2):
                    it = ig * 2 + il
                    ub = il
                    gb = 2 + il
                    self.bank_wait(ub)
                    self.bank_wait(gb)
                    self.W(te, sus, sus.n)
                    self.W(te, sgs, sgs.n)
                    self.W(te, sdve, x2_dv)
                    if x3_guard is not None:
                        self.Wg(te, x3_guard)
                    for ck in range(16):
                        mm = te.matmul(self.bank[ub], us[:, ck, il * 128:(il + 1) * 128],
                                       self.x2[:, ck, :], start=(ck == 0), stop=(ck == 15))
                    upv = spe.inc(mm, 1)
                    for ck in range(16):
                        mm = te.matmul(self.bank[gb], gs[:, ck, il * 128:(il + 1) * 128],
                                       self.x2[:, ck, :], start=(ck == 0), stop=(ck == 15))
                    gpv = spe.inc(mm, 1)
                    if il == 1:
                        uslab_guard[ig % 2] = (spe, upv)
                        gslab_guard[ig % 2] = (spe, gpv)
                    upslot = self.ups[it % 2]
                    self.W(dv, spe, upv)
                    self.Wg(dv, ups_guard[it % 2])
                    ui = dv.tensor_scalar(upslot, self.bank[ub], self.wsu_up[:, it:it + 1],
                                          None, ALU.mult)
                    uv = sdve.inc(ui, 1)
                    self.bank_done(ub, sdve, uv)
                    self.W(ac, spe, gpv)
                    ai = ac.activation(self.x3[:, it, :], self.bank[gb], AF.Silu,
                                       scale=self.wsu_gt[:, it:it + 1])
                    av = sact.inc(ai, 1)
                    self.bank_done(gb, sact, av)
                    self.W(dv, sact, av)
                    xi = dv.scalar_tensor_tensor(self.x3[:, it, :], self.x3[:, it, :],
                                                 self.sudp_c[:, it:it + 1], upslot,
                                                 ALU.mult, ALU.mult)
                    xv = sdve.inc(xi, 1)
                    ups_guard[it % 2] = (sdve, xv)
                    bf_prod[it] = True
                    bf_last[it] = bf_ctr[0]
                    bf_ctr[0] += 1
                    bf_pump()

            bf_last_instr[0] = dv.engine_nop()
            bf_pump(final=True)
            assert all(d == 6 for d in bf_done), bf_done
            ii = dv.engine_nop()
            x3_dv = sdve.inc(ii, 1)
            if self.debug and c == 0:
                self.W(sy, sdve, x2_dv)
                st = sy.dma_start(self.d_x2, self.x2)
                self.s_out.inc(st, 16)
                self.W(sy, sdve, x3_dv)
                st = sy.dma_start(self.d_x3, self.x3)
                self.s_out.inc(st, 16)

            if c + 1 < 8:
                nb2 = (c + 1) // 4
                ntr = ((c + 1) % 4) * 512
                self.W(sy, self.s_sc, self.sc_mark[nb2])
                self.W(sy, spe, self.f2_pe_done)
                for gidx in range(16):
                    ld = sy.dma_start(self.xlnT[:, gidx, :],
                                      self.xln1b[nb2, ntr:ntr + 512,
                                                 gidx * 128:(gidx + 1) * 128],
                                      transpose=True)
                    sld.inc(ld, 16)
                self.xlnT_ready = sld.n

            if self.debug and c == 0:
                self.W(dv, self.s_out, self.s_out.n)
            assert len(bf_order) == 64
            x4_mark = [None] * 64
            for oi, itt in enumerate(bf_order):
                bk = 4 + (oi % 2)
                self.bank_wait(bk)
                self.W(te, sdve, bf_comp[itt])
                mm = te.matmul(self.bank[bk], self.h128s, self.x3[:, itt, :],
                               start=True, stop=True)
                pv = spe.inc(mm, 1)
                self.W(dv, spe, pv)
                ev = dv.tensor_copy(self.x3[:, itt, :], self.bank[bk])
                dvv = sdve.inc(ev, 1)
                x4_mark[itt] = dvv
                self.bank_done(bk, sdve, dvv)

            for jrh in range(2):
                for ti, tau in enumerate(bf_order):
                    ws = self.wdsl[ti % 3]
                    swd = self.s_wd[ti % 3]
                    self.W(sy, self.s_tp, self.prep_done())
                    self.Wg(sy, wd_guard[ti % 3])
                    ld = sy.dma_start(ws[:, 0:1024], self.wdT[tau * 128:(tau + 1) * 128,
                                                               jrh * 1024:(jrh + 1) * 1024])
                    swd.inc(ld, 16)
                    self.W(te, swd, swd.n)
                    self.W(te, sdve, x4_mark[tau])
                    for tt in range(4):
                        for jrl in range(2):
                            if ti == 0:
                                self.bank_wait(tt * 2 + jrl)
                            mm = te.matmul(self.bank[tt * 2 + jrl],
                                           self.x3[:, tau, tt * 128:(tt + 1) * 128],
                                           ws[:, jrl * 512:(jrl + 1) * 512],
                                           start=(ti == 0), stop=(ti == 63))
                    pv = spe.inc(mm, 1)
                    wd_guard[ti % 3] = (spe, pv)
                dn_pe = spe.n
                for tt in range(4):
                    xlr = self.xlres[tt % 2]
                    self.Wg(g, xlres_guard[tt % 2])
                    ldr = g.dma_start(xlr, self.xln1[b, trange + tt * 128:
                                                     trange + (tt + 1) * 128,
                                                     jrh * 1024:(jrh + 1) * 1024])
                    sxl = self.s_xl[tt % 2]
                    rv = sxl.inc(ldr, 16)
                    for jrl in range(2):
                        jcol = jrh * 1024 + jrl * 512
                        bk = tt * 2 + jrl
                        self.W(dv, spe, dn_pe)
                        self.W(dv, sdve, wsdp_ready)
                        for gd in xout_guard:
                            self.Wg(dv, gd)
                        dv.tensor_tensor(self.xout[:, tt, jcol:jcol + 512], self.bank[bk],
                                         self.wsdp_bc[:, jcol:jcol + 512], ALU.mult)
                        self.W(dv, sxl, rv)
                        ei = dv.scalar_tensor_tensor(self.xout[:, tt, jcol:jcol + 512],
                                                     self.xout[:, tt, jcol:jcol + 512], 1.0,
                                                     xlr[:, jrl * 512:(jrl + 1) * 512],
                                                     ALU.bypass, ALU.add)
                        dvv = sdve.inc(ei, 1)
                        self.bank_done(bk, sdve, dvv)
                    xlres_guard[tt % 2] = (sdve, sdve.n)
            xout_guard = []

            vec = self.vec
            for tt in range(4):
                xo = self.xout[:, tt, :]
                gcol = tt * 8
                dv.memset(self.s1c[:, gcol:gcol + 8], 0.0)
                for hh in range(4):
                    dv.scalar_tensor_tensor(self.f6tmp, xo[:, hh * 512:(hh + 1) * 512], 1.0,
                                            xo[:, hh * 512:(hh + 1) * 512],
                                            ALU.bypass, ALU.bypass,
                                            accum_out=self.s1c[:, gcol + hh:gcol + hh + 1])
                    dv.scalar_tensor_tensor(self.f6tmp, xo[:, hh * 512:(hh + 1) * 512],
                                            1.0, xo[:, hh * 512:(hh + 1) * 512],
                                            ALU.bypass, ALU.mult,
                                            accum_out=self.s1c[:, gcol + 4 + hh:gcol + 5 + hh])
            for tt in range(4):
                xo = self.xout[:, tt, :]
                s1 = self.s1c[:, tt * 8:tt * 8 + 8]
                self.dve_edge(dv.scalar_tensor_tensor(vec[:, 5:6], s1[:, 0:1], 1.0,
                                                      s1[:, 1:2], ALU.bypass, ALU.add))
                self.dve_edge(dv.scalar_tensor_tensor(vec[:, 6:7], s1[:, 2:3], 1.0,
                                                      s1[:, 3:4], ALU.bypass, ALU.add))
                self.dve_edge(dv.tensor_tensor(vec[:, 0:1], vec[:, 5:6], vec[:, 6:7],
                                               ALU.add))
                self.dve_edge(dv.scalar_tensor_tensor(vec[:, 5:6], s1[:, 4:5], 1.0,
                                                      s1[:, 5:6], ALU.bypass, ALU.add))
                self.dve_edge(dv.scalar_tensor_tensor(vec[:, 6:7], s1[:, 6:7], 1.0,
                                                      s1[:, 7:8], ALU.bypass, ALU.add))
                self.dve_edge(dv.tensor_tensor(vec[:, 1:2], vec[:, 5:6], vec[:, 6:7],
                                               ALU.add))
                self.dve_edge(dv.tensor_scalar(vec[:, 0:1], vec[:, 0:1], 1.0 / 2048.0,
                                               None, ALU.mult))
                self.dve_edge(dv.tensor_scalar(vec[:, 1:2], vec[:, 1:2], 1.0 / 2048.0,
                                               None, ALU.mult))
                self.dve_edge(dv.scalar_tensor_tensor(vec[:, 2:3], vec[:, 0:1], 1.0,
                                                      vec[:, 0:1], ALU.bypass, ALU.mult))
                si = dv.tensor_tensor(vec[:, 1:2], vec[:, 1:2], vec[:, 2:3], ALU.subtract)
                sv = sdve.inc(si, 1)
                self.W(ac, sdve, sv)
                a0 = ac.activation(vec[:, 3:4], vec[:, 1:2], AF.Ln, bias=self.biases[:, 0:1],
                                   scale=1.0)
                v0 = sact.inc(a0, 1)
                self.W(ac, sact, v0)
                ai = ac.activation(vec[:, 3:4], vec[:, 3:4], AF.Exp, scale=-0.5)
                av = sact.inc(ai, 1)
                self.W(dv, sact, av)
                ni = dv.tensor_scalar(xo, xo, vec[:, 0:1], vec[:, 3:4],
                                      ALU.subtract, ALU.mult)
                nv = sdve.inc(ni, 1)
                self.W(g, sdve, nv)
                st = g.dma_start(self.out[b, trange + tt * 128:trange + (tt + 1) * 128, :], xo)
                self.s_out.inc(st, 16)
            xout_guard = [(self.s_out, self.s_out.n)]
            x2_guard = (spe, spe.n)
            x3_guard = (spe, spe.n)
            if c + 1 < 8:
                self.hs2_pre = {}
                for gidx in range(2):
                    self.Wg(dv, hs2_guard[gidx % 2])
                    hi = dv.tensor_scalar(self.hs2[gidx % 2], self.h128f,
                                          self.suug_c[:, gidx:gidx + 1], C2,
                                          ALU.mult, ALU.mult)
                    self.hs2_pre[gidx] = sdve.inc(hi, 1)

    def build(self):
        self.emit_init()
        for b in range(NB):
            self.emit_A(b)
        self.emit_F()
        self.nc.sync.wait_ge(self.s_out.h, self.s_out.n)
        return self.nc


def _consts():
    h = np.array([[1.0]], dtype=np.float32)
    for _ in range(7):
        h = np.block([[h, h], [h, -h]])
    i = np.arange(128)
    tri = (i[:, None] <= i[None, :]).astype(ml_dtypes.bfloat16)
    eye8 = np.eye(8, dtype=np.float32)
    onesrow = np.ones((1, 128), dtype=np.float32)
    return h.astype(np.float32), tri, eye8, onesrow


def kernel(L0_attn_hidden_states, L0_attn_Wq, L0_attn_Wk, L0_attn_Wv, L0_attn_Wg,
           L0_attn_Wc, L0_attn_bc, L0_ffn_SU_ug, L0_ffn_SU_dp, L0_ffn_Wscale_ug,
           L0_ffn_Wscale_dp, L0_ffn_W_up, L0_ffn_W_gate, L0_ffn_W_down):
    h128, tri, eye8, onesrow = _consts()
    bld = B(debug=False)
    nc = bld.build()
    x = np.ascontiguousarray(np.asarray(L0_attn_hidden_states, np.float32))
    cnp = lambda a: np.ascontiguousarray(np.asarray(a, np.float32))
    shared = dict(
        wq=cnp(L0_attn_Wq), wk=cnp(L0_attn_Wk), wv=cnp(L0_attn_Wv), wgi=cnp(L0_attn_Wg),
        wc=cnp(L0_attn_Wc), bc=cnp(L0_attn_bc).reshape(1, C),
        su_ug=cnp(L0_ffn_SU_ug).reshape(1, C), su_dp=cnp(L0_ffn_SU_dp).reshape(1, INTER),
        wsug=cnp(L0_ffn_Wscale_ug).reshape(1, 2 * INTER),
        wsdp=cnp(L0_ffn_Wscale_dp).reshape(1, C),
        wup=cnp(L0_ffn_W_up), wgate=cnp(L0_ffn_W_gate), wdown=cnp(L0_ffn_W_down),
        h128=h128, tri=tri, eye8=eye8, onesrow=onesrow,
    )
    in_maps = []
    for cidx in range(8):
        mm = dict(shared)
        mm["xin"] = np.ascontiguousarray(x[2 * cidx:2 * cidx + 2])
        in_maps.append(mm)
    res = run_bass_kernel_spmd(nc, in_maps, core_ids=list(range(8)))
    return np.concatenate([np.asarray(r["out"], np.float32) for r in res.results], axis=0)


# version 15
# speedup vs baseline: 1.0006x; 1.0006x over previous
import math
import numpy as np
import ml_dtypes
import concourse.bass as bass
import concourse.mybir as mybir
from concourse.bass_utils import run_bass_kernel_spmd

F32 = mybir.dt.float32
BF16 = mybir.dt.bfloat16
AF = mybir.ActivationFunctionType
ALU = mybir.AluOpType
AX = mybir.AxisListType

T = 2048
C = 2048
NB = 2
H = 8
HQ = 16
INTER = 8192
GATE_OFFSET = 6.906768
C2 = 1.0 / (64.0 * math.sqrt(2048.0))
CD = 1.0 / (64.0 * math.sqrt(8192.0))
BIAS_EP = -0.25 * math.log(128.0)
SBUF_TOP = 229344


class Sem:
    def __init__(self, nc, name):
        self.h = nc.semaphore(name).__enter__()
        self.n = 0

    def inc(self, instr, k):
        instr.then_inc(self.h, k)
        self.n += k
        return self.n


class B:
    def __init__(self, debug=False):
        self.debug = debug
        nc = bass.Bass("TRN2", target_bir_lowering=False)
        self.nc = nc
        self.wm = {}

        dt = nc.dram_tensor
        self.xin = dt("xin", [NB, T, C], F32, kind="ExternalInput").ap()
        self.wq = dt("wq", [C, C], F32, kind="ExternalInput").ap()
        self.wk = dt("wk", [C, 1024], F32, kind="ExternalInput").ap()
        self.wv = dt("wv", [C, 1024], F32, kind="ExternalInput").ap()
        self.wgi = dt("wgi", [C, H], F32, kind="ExternalInput").ap()
        self.wc = dt("wc", [C, C], F32, kind="ExternalInput").ap()
        self.bc = dt("bc", [1, C], F32, kind="ExternalInput").ap()
        self.su_ug = dt("su_ug", [1, C], F32, kind="ExternalInput").ap()
        self.su_dp = dt("su_dp", [1, INTER], F32, kind="ExternalInput").ap()
        self.wsug = dt("wsug", [1, 2 * INTER], F32, kind="ExternalInput").ap()
        self.wsdp = dt("wsdp", [1, C], F32, kind="ExternalInput").ap()
        self.wup = dt("wup", [INTER, C], F32, kind="ExternalInput").ap()
        self.wgate = dt("wgate", [INTER, C], F32, kind="ExternalInput").ap()
        self.wdown = dt("wdown", [C, INTER], F32, kind="ExternalInput").ap()
        self.h128 = dt("h128", [128, 128], F32, kind="ExternalInput").ap()
        self.tri = dt("tri", [128, 128], BF16, kind="ExternalInput").ap()
        self.eye8 = dt("eye8", [8, 8], F32, kind="ExternalInput").ap()
        self.onesrow = dt("onesrow", [1, 128], F32, kind="ExternalInput").ap()

        self.out = dt("out", [NB, T, C], F32, kind="ExternalOutput").ap()

        self.xbf = dt("xbf", [NB, T, C], BF16, kind="Internal").ap()
        self.wq_b = dt("wq_b", [C, C], BF16, kind="Internal").ap()
        self.wk_b = dt("wk_b", [C, 1024], BF16, kind="Internal").ap()
        self.wv_b = dt("wv_b", [C, 1024], BF16, kind="Internal").ap()
        self.wg_b = dt("wg_b", [C, H], BF16, kind="Internal").ap()
        self.wc_b = dt("wc_b", [C, C], BF16, kind="Internal").ap()
        self.wup_b = dt("wup_b", [INTER, C], BF16, kind="Internal").ap()
        self.wgate_b = dt("wgate_b", [INTER, C], BF16, kind="Internal").ap()
        self.wdown_b = dt("wdown_b", [C, INTER], BF16, kind="Internal").ap()
        self.wupT = dt("wupT", [C, INTER], BF16, kind="Internal").ap()
        self.wgateT = dt("wgateT", [C, INTER], BF16, kind="Internal").ap()
        self.wdT = dt("wdT", [INTER, C], BF16, kind="Internal").ap()
        xk = "ExternalOutput" if debug else "Internal"
        self.xln1 = dt("xln1", [NB, T, C], F32, kind=xk).ap()
        self.xln1b = dt("xln1b", [NB, T, C], BF16, kind="Internal").ap()
        if debug:
            self.d_qt = dt("d_qt", [NB, 128, HQ, T], BF16, kind="ExternalOutput").ap()
            self.d_kt = dt("d_kt", [NB, 128, H, T], BF16, kind="ExternalOutput").ap()
            self.d_v = dt("d_v", [NB, 128, 16, 1024], BF16, kind="ExternalOutput").ap()
            self.d_y = dt("d_y", [NB, 128, HQ, T], BF16, kind="ExternalOutput").ap()
            self.d_x2 = dt("d_x2", [128, 16, 512], BF16, kind="ExternalOutput").ap()
            self.d_x3 = dt("d_x3", [128, 64, 512], BF16, kind="ExternalOutput").ap()
            self.d_vec = dt("d_vec", [16, 128, 8], F32, kind="ExternalOutput").ap()
            self.d_s1c = dt("d_s1c", [16, 128, 8], F32, kind="ExternalOutput").ap()
            self.d_wsbc = dt("d_wsbc", [128, C], F32, kind="ExternalOutput").ap()

        base = (SBUF_TOP - nc.sbuf_bytes_remaining + 31) & ~31
        o = {
            "big0": base,
            "big12": base + 65536,
            "big3": base + 131072,
            "sl": base + 163840,
            "wring": base + 180224,
            "bcr": base + 192512,
            "misc": base + 200704,
        }
        self.o = o

        def at(name, shape, dtp, off):
            return nc.alloc_sbuf_tensor_at(name, shape, dtp, offset=off).ap()

        self.qt = at("qt", [128, HQ, T], BF16, o["big0"])
        self.kt = at("kt", [128, H, T], BF16, o["big12"])
        self.v = at("v", [128, 16, 1024], BF16, o["big12"] + 32768)
        self.wc_sb = at("wc_sb", [128, 16, C], BF16, o["big12"])
        self.xTq2 = [at("xTq", [128, 16, 512], BF16, o["big3"]),
                     at("xTqb", [128, 16, 512], BF16, o["big3"] + 16384)]
        self.xres = [at(f"xres{i}", [128, C], F32, o["big3"] + 16384 + 8192 * i) for i in range(2)]
        self.slab = [at(f"slab{i}", [128, 16, 256], BF16, o["sl"] + 8192 * i) for i in range(2)]
        self.a_sb = at("a_sb", [128, 8, 512], BF16, o["sl"])
        self.bfst = at("bfst", [128, C], BF16, o["sl"] + 8192)
        self.stg = [at(f"stg{i}", [128, 1024], BF16, o["wring"] + 2048 * i) for i in range(2)]
        self.epen = [at(f"epen{i}", [128, 512], F32, o["wring"] + 4096 + 2048 * i) for i in range(4)]
        self.rb_sb = at("rb_sb", [128, 512], F32, o["wring"] + 4096)
        self.nrm_t = at("nrm_t", [128, 512], F32, o["wring"] + 6144)
        self.x3 = at("x3", [128, 64, 512], BF16, o["big0"])
        self.uslab = [at(f"uslab{i}", [128, 16, 256], BF16, o["big12"] + 8192 * i) for i in range(2)]
        self.gslab = [at(f"gslab{i}", [128, 16, 256], BF16, o["big12"] + 16384 + 8192 * i) for i in range(2)]
        self.xlnT = at("xlnT", [128, 16, 512], BF16, o["big12"] + 32768)
        self.x2 = at("x2", [128, 16, 512], BF16, o["big12"] + 49152)
        self.xout = at("xout", [128, 4, C], F32, o["big3"])
        self.xlres = [at(f"xlres{i}", [128, 1024], F32, o["sl"] + 4096 * i) for i in range(2)]
        self.btmp = at("btmp", [128, 512], BF16, o["sl"] + 8192)
        self.ups = [at(f"ups{i}", [128, 512], BF16, o["sl"] + 9216 + 1024 * i) for i in range(2)]
        self.f6tmp = at("f6tmp", [128, 512], F32, o["sl"] + 11264)
        self.wdsl = [at(f"wdsl{i}", [128, 2048], BF16, o["wring"] + 4096 * i) for i in range(3)]
        self.bc_bc = at("bc_bc", [128, C], F32, o["bcr"])
        self.wsdp_bc = at("wsdp_bc", [128, C], F32, o["bcr"])
        m = o["misc"]
        def mk(name, shape, dtp, nbytes):
            nonlocal m
            t = at(name, shape, dtp, m)
            m += nbytes
            return t
        self.eye8_sb = mk("eye8_sb", [8, 8], F32, 32)
        self.h128f = mk("h128f", [128, 128], F32, 512)
        self.h128s = mk("h128s", [128, 128], BF16, 256)
        self.tri_sb = mk("tri_sb", [128, 128], BF16, 256)
        self.suug_c = mk("suug_c", [128, 16], F32, 64)
        self.sudp_c = mk("sudp_c", [128, 64], F32, 256)
        self.wsu_up = mk("wsu_up", [128, 64], F32, 256)
        self.wsu_gt = mk("wsu_gt", [128, 64], F32, 256)
        self.wg_sb = mk("wg_sb", [128, 16, 8], BF16, 256)
        self.ones_r = mk("ones_r", [1, 128], F32, 512)
        self.onescol = mk("onescol", [128, 1], BF16, 32)
        self.biases = mk("biases", [128, 8], F32, 32)
        self.gbuf0 = mk("gbuf0", [128, 512], F32, 2048)
        self.gbuf1 = mk("gbuf1", [128, 512], F32, 2048)
        self.r0row = mk("r0row", [1, 512], F32, 2048)
        self.r0row2 = mk("r0row2", [1, 512], F32, 2048)
        self.hs2 = [mk(f"hs2_{i}", [128, 128], BF16, 256) for i in range(2)]
        self.s1c = mk("s1c", [128, 32], F32, 128)
        self.vec = mk("vec", [128, 8], F32, 32)
        assert m <= SBUF_TOP, f"sbuf overflow: {m} > {SBUF_TOP}"
        self.misc_end = m

        self.pp = [nc.alloc_psum_tensor(f"pp{i}", [128, 1024], F32).ap() for i in range(4)]
        self.bank = [self.pp[i // 2][:, (i % 2) * 512:(i % 2 + 1) * 512] for i in range(8)]
        self.bank_guard = [None] * 8

        self.s_ld = Sem(nc, "s_ld")
        self.s_tp = Sem(nc, "s_tp")
        self.s_gp = Sem(nc, "s_gp")
        self.s_cast = Sem(nc, "s_cast")
        self.s_sc = Sem(nc, "s_sc")
        self.s_out = Sem(nc, "s_out")
        self.s_sl = [Sem(nc, "s_sl0"), Sem(nc, "s_sl1")]
        self.s_us = [Sem(nc, "s_us0"), Sem(nc, "s_us1")]
        self.s_gs = [Sem(nc, "s_gs0"), Sem(nc, "s_gs1")]
        self.s_wd = [Sem(nc, "s_wd0"), Sem(nc, "s_wd1"), Sem(nc, "s_wd2")]
        self.s_xq = [Sem(nc, "s_xq0"), Sem(nc, "s_xq1")]
        self.s_xr = [Sem(nc, "s_xr0"), Sem(nc, "s_xr1")]
        self.s_xl = [Sem(nc, "s_xl0"), Sem(nc, "s_xl1")]
        self.s_wc = Sem(nc, "s_wc")
        self.s_pe = Sem(nc, "s_pe")
        self.s_dve = Sem(nc, "s_dve")
        self.s_act = Sem(nc, "s_act")

        self.prep_q = []
        self.prep_i = 0
        self.cast_marks = {}
        self.slab_guard = [None, None]
        self.xTq_guard = [None, None]
        self.sc_mark = [0, 0]
        self.bfst_guard = 0

    def bcast_row(self, jr):
        return self.gbuf0[0:1, :]

    def W(self, eng, sem, val):
        if val is None or val <= 0:
            return
        key = (id(eng), id(sem))
        if self.wm.get(key, 0) >= val:
            return
        eng.wait_ge(sem.h, val)
        self.wm[key] = val

    def Wg(self, eng, guard):
        if guard is not None:
            self.W(eng, guard[0], guard[1])

    def dve_edge(self, instr):
        v = self.s_dve.inc(instr, 1)
        self.W(self.nc.vector, self.s_dve, v)
        return v

    def bank_wait(self, i):
        self.Wg(self.nc.tensor, self.bank_guard[i])

    def bank_done(self, i, sem, val):
        self.bank_guard[i] = (sem, val)

    def emit_init(self):
        nc = self.nc
        g, dv, te = nc.gpsimd, nc.vector, nc.tensor
        scast, sgp = self.s_cast, self.s_gp

        def cast(dst, src, name, parts=1):
            n = src.shape[0]
            step = n // parts
            for i in range(parts):
                g.dma_start(dst[i * step:(i + 1) * step], src[i * step:(i + 1) * step])\
                    .then_inc(scast.h, 16)
                scast.n += 16
            self.cast_marks[name] = scast.n
            g.wait_ge(scast.h, scast.n)

        cast(self.xbf[0], self.xin[0], "x0", parts=2)
        cast(self.wg_b, self.wgi, "wg")
        cast(self.wq_b, self.wq, "wq", parts=2)
        cast(self.wk_b, self.wk, "wk")
        cast(self.wv_b, self.wv, "wv")
        cast(self.xbf[1], self.xin[1], "x1", parts=2)
        cast(self.wc_b, self.wc, "wc", parts=2)
        cast(self.wup_b, self.wup, "wup", parts=4)
        cast(self.wgate_b, self.wgate, "wgate", parts=4)
        cast(self.wdown_b, self.wdown, "wdown", parts=4)

        def gload(dst, src, slow=False):
            g.dma_start(dst, src, allow_slow_non_contiguous=slow).then_inc(sgp.h, 16)
            sgp.n += 16

        gload(self.eye8_sb, self.eye8)
        gload(self.h128f, self.h128)
        gload(self.tri_sb, self.tri)
        gload(self.ones_r, self.onesrow)
        gload(self.suug_c, self.su_ug.rearrange("o (a p) -> (o p) a", p=128), slow=True)
        gload(self.sudp_c, self.su_dp.rearrange("o (a p) -> (o p) a", p=128), slow=True)
        gload(self.wsu_up, self.wsug[:, 0:INTER].rearrange("o (a p) -> (o p) a", p=128), slow=True)
        gload(self.wsu_gt, self.wsug[:, INTER:].rearrange("o (a p) -> (o p) a", p=128), slow=True)
        self.mark_gp_small = sgp.n
        g.wait_ge(sgp.h, sgp.n)
        self.W(g, scast, self.cast_marks["wg"])
        gload(self.wg_sb, self.wg_b.rearrange("(ck p) h -> p ck h", p=128), slow=True)
        self.mark_wg = sgp.n
        g.wait_ge(sgp.h, sgp.n)

        dv.memset(self.biases[:, 0:1], 1e-5)
        dv.memset(self.biases[:, 1:2], 1.0)
        dv.memset(self.biases[:, 2:3], -GATE_OFFSET)
        dv.memset(self.biases[:, 3:4], BIAS_EP)
        dv.memset(self.onescol, 1.0)
        dv.memset(self.gbuf1[32:40, 0:1], 0.0)
        self.W(dv, sgp, self.mark_gp_small)
        dv.tensor_scalar(self.wsu_up, self.wsu_up, 64.0, None, ALU.mult)
        dv.tensor_scalar(self.wsu_gt, self.wsu_gt, 64.0, None, ALU.mult)
        ii = dv.tensor_scalar(self.h128s, self.h128f, CD, None, ALU.mult)
        self.s_dve.inc(ii, 1)
        self.mark_dve_init = self.s_dve.n

        self.W(te, sgp, self.mark_gp_small)
        for jr in range(4):
            self.W(g, self.s_pe, self.s_pe.n)
            gload(self.bcast_row(jr), self.bc[:, jr * 512:(jr + 1) * 512])
            bk = 6 + (jr % 2)
            self.bank_wait(bk)
            self.W(te, sgp, sgp.n)
            mm = te.matmul(self.bank[bk], self.ones_r[0:1, :], self.bcast_row(jr),
                           start=True, stop=True)
            pv = self.s_pe.inc(mm, 1)
            self.W(dv, self.s_pe, pv)
            cp = dv.tensor_copy(self.bc_bc[:, jr * 512:(jr + 1) * 512], self.bank[bk])
            dvv = self.s_dve.inc(cp, 1)
            self.bank_done(bk, self.s_dve, dvv)
        self.mark_bcbc = self.s_dve.n

        for (nat, castm, dst, rows, cols) in [
            (self.wup_b, "wup", self.wupT, INTER, C),
            (self.wgate_b, "wgate", self.wgateT, INTER, C),
            (self.wdown_b, "wdown", self.wdT, C, INTER),
        ]:
            for cc in range(cols // 128):
                for rc in range(rows // 1024):
                    self.prep_q.append((nat, castm, dst, cc, rc))

    def prep_pump(self, k):
        sy = self.nc.sync
        n = 0
        while n < k and self.prep_i < len(self.prep_q):
            i = self.prep_i
            nat, castm, dst, cc, rc = self.prep_q[i]
            buf = self.stg[i % 2]
            self.W(sy, self.s_cast, self.cast_marks[castm])
            if i >= 2:
                self.W(sy, self.s_tp, 16 * (2 * (i - 2) + 2))
            ld = sy.dma_start(buf, nat[rc * 1024:(rc + 1) * 1024, cc * 128:(cc + 1) * 128],
                              transpose=True)
            self.s_tp.inc(ld, 16)
            self.W(sy, self.s_tp, 16 * (2 * i + 1))
            st = sy.dma_start(dst[cc * 128:(cc + 1) * 128, rc * 1024:(rc + 1) * 1024], buf)
            self.s_tp.inc(st, 16)
            self.prep_i += 1
            n += 1

    def prep_done(self):
        return 16 * 2 * len(self.prep_q)

    def emit_A(self, b):
        nc = self.nc
        te, dv, ac, sy, g = nc.tensor, nc.vector, nc.scalar, nc.sync, nc.gpsimd
        spe, sdve, sact, sld, sgp = self.s_pe, self.s_dve, self.s_act, self.s_ld, self.s_gp

        xbf_b = self.xbf[b]
        gq = self.gbuf0[0:8, :]
        e1q = self.gbuf0[32:40, :]
        tqb = [self.gbuf0[64:72, :], self.gbuf1[0:8, :]]
        off8 = self.gbuf1[32:40, 0:1]

        self.W(dv, sdve, self.mark_dve_init)
        if b > 0:
            dv.memset(self.gbuf1[32:40, 0:1], 0.0)

        slab_guard = self.slab_guard
        epen_guard = getattr(self, "epen_guard_state", [None] * 4)
        self.epen_guard_state = epen_guard
        ep_idx = getattr(self, "ep_idx_state", 0)
        wqr = self.wq_b.rearrange("(ck p) j -> p ck j", p=128)
        wkr = self.wk_b.rearrange("(ck p) j -> p ck j", p=128)
        wvr = self.wv_b.rearrange("(ck p) j -> p ck j", p=128)

        for tq in range(4):
            t0 = tq * 512
            xTq = self.xTq2[tq % 2]
            self.W(sy, self.s_cast, self.cast_marks[f"x{b}"])
            for gd in (self.xTq_guard[tq % 2] or []):
                self.Wg(sy, gd)
            sxq = self.s_xq[tq % 2]
            for gidx in range(16):
                ld = sy.dma_start(xTq[:, gidx, :],
                                  xbf_b[t0:t0 + 512, gidx * 128:(gidx + 1) * 128],
                                  transpose=True)
                sxq.inc(ld, 16)
            xT_ready = (sxq, sxq.n)

            self.bank_wait(4)
            self.W(te, xT_ready[0], xT_ready[1])
            self.W(te, sgp, self.mark_wg)
            for ck in range(16):
                mm = te.matmul(self.bank[4][0:8, :], self.wg_sb[:, ck, :], xTq[:, ck, :],
                               start=(ck == 0), stop=(ck == 15))
            gpe = spe.inc(mm, 1)
            self.W(dv, spe, gpe)
            cp = dv.tensor_copy(gq, self.bank[4][0:8, :])
            gdv = sdve.inc(cp, 1)
            self.bank_done(4, sdve, gdv)
            self.W(ac, sdve, gdv)
            a0 = ac.activation(e1q, gq, AF.Exp, bias=self.biases[0:8, 2:3], scale=-1.0)
            v0 = sact.inc(a0, 1)
            self.W(ac, sact, v0)
            ai = ac.activation(tqb[0], e1q, AF.Ln, bias=self.biases[0:8, 1:2], scale=1.0)
            av = sact.inc(ai, 1)
            self.W(dv, sact, av)
            cur = 0
            k = 1
            while k < 512:
                src, dst2 = tqb[cur], tqb[1 - cur]
                dv.tensor_tensor(dst2[:, k:512], src[:, k:512], src[:, 0:512 - k], ALU.add)
                self.dve_edge(dv.tensor_copy(dst2[:, 0:k], src[:, 0:k]))
                cur = 1 - cur
                k *= 2
            L1cq = tqb[cur]
            assert cur == 1, "L1cq must land at base partition 0 for the eye8 matmul"
            self.dve_edge(dv.tensor_scalar(L1cq, L1cq, off8, None, ALU.add))
            ii = dv.tensor_copy(off8, L1cq[:, 511:512])
            cum_dv = sdve.inc(ii, 1)
            self.W(dv, sdve, cum_dv)

            for s in range(4):
                sb = self.slab[s % 2]
                ssl = self.s_sl[s % 2]
                self.W(sy, self.s_cast, self.cast_marks["wv"])
                self.Wg(sy, slab_guard[s % 2])
                ld = sy.dma_start(sb, wvr[:, :, s * 256:(s + 1) * 256])
                ssl.inc(ld, 16)
                self.W(te, ssl, ssl.n)
                for tt in range(4):
                    bk = tt % 2
                    self.bank_wait(bk)
                    for ck in range(16):
                        mm = te.matmul(self.bank[bk][:, 0:256],
                                       xTq[:, ck, tt * 128:(tt + 1) * 128],
                                       sb[:, ck, :], start=(ck == 0), stop=(ck == 15))
                    pv = spe.inc(mm, 1)
                    if tt == 3:
                        slab_guard[s % 2] = (spe, pv)
                    self.W(dv, spe, pv)
                    ev = dv.tensor_copy(self.v[:, tq * 4 + tt, s * 256:(s + 1) * 256],
                                        self.bank[bk][:, 0:256])
                    dvv = sdve.inc(ev, 1)
                    self.bank_done(bk, sdve, dvv)

            self.W(te, sdve, cum_dv)
            r0 = [self.r0row, self.r0row2]
            r0_guard = getattr(self, "r0_guard", [None, None])
            self.r0_guard = r0_guard
            self.bank_wait(5)
            mm = te.matmul(self.bank[5][0:1, :], self.eye8_sb[:, 0:1], L1cq,
                           start=True, stop=True)
            pv = spe.inc(mm, 1)
            self.W(dv, spe, pv)
            self.Wg(dv, r0_guard[0])
            cpy = dv.tensor_copy(r0[0], self.bank[5][0:1, :])
            dvv = sdve.inc(cpy, 1)
            self.bank_done(5, sdve, dvv)
            copy_dv = dvv
            for h in range(H):
                self.bank_wait(6)
                self.W(te, sdve, copy_dv)
                mm = te.matmul(self.bank[6], self.ones_r[0:1, :], r0[h % 2],
                               start=True, stop=True)
                pv = spe.inc(mm, 1)
                r0_guard[h % 2] = (spe, pv)
                iep = ep_idx % 4
                ien = (ep_idx + 1) % 4
                ep_idx += 2
                self.W(ac, spe, pv)
                self.Wg(ac, epen_guard[iep])
                ac.activation(self.epen[iep], self.bank[6], AF.Exp,
                              bias=self.biases[:, 3:4], scale=-0.5)
                self.Wg(ac, epen_guard[ien])
                ai = ac.activation(self.epen[ien], self.bank[6], AF.Exp,
                                   bias=self.biases[:, 3:4], scale=0.5)
                av = sact.inc(ai, 1)
                self.bank_done(6, sact, av)
                if h + 1 < H:
                    self.bank_wait(5)
                    mm = te.matmul(self.bank[5][0:1, :], self.eye8_sb[:, h + 1:h + 2],
                                   L1cq, start=True, stop=True)
                    pv = spe.inc(mm, 1)
                    self.W(dv, spe, pv)
                    self.Wg(dv, r0_guard[(h + 1) % 2])
                    cpy = dv.tensor_copy(r0[(h + 1) % 2], self.bank[5][0:1, :])
                    copy_dv = sdve.inc(cpy, 1)
                    self.bank_done(5, sdve, copy_dv)

                sb = self.slab[h % 2]
                ssl = self.s_sl[h % 2]
                self.W(sy, self.s_cast, self.cast_marks["wq"])
                self.Wg(sy, slab_guard[h % 2])
                ld = sy.dma_start(sb, wqr[:, :, h * 256:(h + 1) * 256])
                ssl.inc(ld, 16)
                self.W(te, ssl, ssl.n)
                for jl in range(2):
                    jt = 2 * h + jl
                    bk = jl
                    self.bank_wait(bk)
                    for ck in range(16):
                        mm = te.matmul(self.bank[bk], sb[:, ck, jl * 128:(jl + 1) * 128],
                                       xTq[:, ck, :], start=(ck == 0), stop=(ck == 15))
                    pv = spe.inc(mm, 1)
                    if jl == 1:
                        slab_guard[h % 2] = (spe, pv)
                    self.W(dv, spe, pv)
                    self.W(dv, sact, av)
                    ev = dv.tensor_tensor(self.qt[:, jt, t0:t0 + 512], self.bank[bk],
                                          self.epen[iep], ALU.mult)
                    dvv = sdve.inc(ev, 1)
                    self.bank_done(bk, sdve, dvv)
                sb2 = self.slab[(h + 1) % 2]
                ssl2 = self.s_sl[(h + 1) % 2]
                self.W(sy, self.s_cast, self.cast_marks["wk"])
                self.Wg(sy, slab_guard[(h + 1) % 2])
                ld = sy.dma_start(sb2[:, :, 0:128], wkr[:, :, h * 128:(h + 1) * 128])
                ssl2.inc(ld, 16)
                self.W(te, ssl2, ssl2.n)
                self.bank_wait(2)
                for ck in range(16):
                    mm = te.matmul(self.bank[2], sb2[:, ck, 0:128], xTq[:, ck, :],
                                   start=(ck == 0), stop=(ck == 15))
                pv = spe.inc(mm, 1)
                slab_guard[(h + 1) % 2] = (spe, pv)
                self.W(dv, spe, pv)
                ev = dv.tensor_tensor(self.kt[:, h, t0:t0 + 512], self.bank[2],
                                      self.epen[ien], ALU.mult)
                dvv = sdve.inc(ev, 1)
                self.bank_done(2, sdve, dvv)
                epen_guard[iep] = (sdve, sdve.n)
                epen_guard[ien] = (sdve, dvv)
            self.xTq_guard[tq % 2] = [(spe, spe.n)]

        self.ep_idx_state = ep_idx
        proj_dve = sdve.n
        if self.debug:
            self.W(sy, sdve, proj_dve)
            for dst, src in [(self.d_qt[b], self.qt), (self.d_kt[b], self.kt),
                             (self.d_v[b], self.v)]:
                st = sy.dma_start(dst, src)
                self.s_out.inc(st, 16)
            dbg_mark = self.s_out.n

        self.W(te, sdve, proj_dve)
        a_guard = [None] * 8
        ag0 = slab_guard[0][1] if slab_guard[0] else 0
        ag1 = slab_guard[1][1] if slab_guard[1] else 0
        for _i in range(4):
            a_guard[_i] = (spe, ag0)
            a_guard[4 + _i] = (spe, ag1)
        self.bfst_slab1_guard = (spe, ag1)
        qk_pair = 0
        first_y = True
        pending = []

        def fin_group(p):
            nonlocal first_y
            (phq, ptcol0, pyb, pnb, pypv) = p
            tmp = self.nrm_t[0:1, :]
            rec = self.rb_sb[0:1, :]
            self.W(dv, spe, pypv)
            self.dve_edge(dv.tensor_scalar(tmp, self.bank[pnb][0:1, :], 1e-6, None, ALU.add))
            ri = dv.reciprocal(rec, tmp)
            rdv = sdve.inc(ri, 1)
            self.W(te, sdve, rdv)
            mm = te.matmul(self.bank[pnb], self.ones_r[0:1, :], rec, start=True, stop=True)
            pv = spe.inc(mm, 1)
            self.W(dv, spe, pv)
            cpy = dv.tensor_copy(self.rb_sb, self.bank[pnb])
            dvv = self.dve_edge(cpy)
            self.bank_done(pnb, sdve, dvv)
            if first_y and self.debug:
                self.W(dv, self.s_out, dbg_mark)
            first_y = False
            ev = dv.tensor_tensor(self.qt[:, phq, ptcol0:ptcol0 + 512], self.bank[pyb],
                                  self.rb_sb, ALU.mult)
            dvv = sdve.inc(ev, 1)
            self.bank_done(pyb, sdve, dvv)

        class Grp:
            def __init__(g, hq, tr, sub):
                g.hq = hq
                g.h = hq // 2
                g.tr = tr
                g.tcol0 = tr * 512
                g.S = 4 * tr + 4
                g.sub = sub
                g.yb = 4 + sub
                g.nb = 6 + sub
                g.st = 0
                g.chunk_par = 0
                g.pend = None

        def emit_qk(g):
            fused = (g.st + 1 < 4 * g.tr)
            nst = 2 if fused else 1
            pq = qk_pair[0] % 2
            qk_pair[0] += 1
            pairp = self.pp[pq]
            los = []
            for u in range(nst):
                sti = g.st + u
                kk = sti - 4 * g.tr
                lo = 0 if kk < 0 else 128 * kk
                los.append(lo)
                self.bank_wait(2 * pq + u)
                mm = te.matmul(pairp[:, u * 512 + lo:(u + 1) * 512],
                               self.kt[:, g.h, sti * 128:(sti + 1) * 128],
                               self.qt[:, g.hq, g.tcol0 + lo:g.tcol0 + 512],
                               start=True, stop=True)
            qkv_ = spe.inc(mm, 1)
            slot = g.sub * 4 + g.chunk_par * 2
            g.chunk_par ^= 1
            self.W(ac, spe, qkv_)
            for u in range(nst):
                self.Wg(ac, a_guard[slot + u])
            if fused:
                ai = ac.activation(
                    self.a_sb[:, slot:slot + 2, :].rearrange("p a b -> p (a b)"),
                    pairp, AF.Square)
            else:
                lo = los[0]
                ai = ac.activation(self.a_sb[:, slot, lo:512], pairp[:, lo:512], AF.Square)
            av = sact.inc(ai, 1)
            for u in range(nst):
                self.bank_done(2 * pq + u, sact, av)
            if fused:
                g.pend = (slot, nst, los, g.st, "act", av)
            else:
                lo = los[0]
                self.W(dv, sact, av)
                mi = dv.tensor_tensor(self.a_sb[:, slot, lo:lo + 128],
                                      self.a_sb[:, slot, lo:lo + 128],
                                      self.tri_sb, ALU.mult)
                dvv = sdve.inc(mi, 1)
                g.pend = (slot, nst, los, g.st, "dve", dvv)
            g.st += nst

        def emit_yn(g):
            (slot, nst, los, st0, wk, wv) = g.pend
            g.pend = None
            if wk == "act":
                self.W(te, sact, wv)
            else:
                self.W(te, sdve, wv)
            for u in range(nst):
                sti = st0 + u
                lo = los[u]
                te.matmul(self.bank[g.yb][:, lo:512],
                          self.v[:, sti, g.h * 128:(g.h + 1) * 128],
                          self.a_sb[:, slot + u, lo:512],
                          start=(sti == 0), stop=(sti == g.S - 1))
                mm2 = te.matmul(self.bank[g.nb][0:1, lo:512], self.onescol,
                                self.a_sb[:, slot + u, lo:512],
                                start=(sti == 0), stop=(sti == g.S - 1))
                pv = spe.inc(mm2, 1)
                a_guard[slot + u] = (spe, pv)

        qk_pair = [0]
        pairs = [(hq, tr) for hq in range(HQ) for tr in range(4)]
        for gi in range(0, len(pairs), 2):
            g0 = Grp(*pairs[gi], 0)
            g1 = Grp(*pairs[gi + 1], 1)
            for p in pending:
                fin_group(p)
            pending = []
            self.bank_wait(g0.yb)
            self.bank_wait(g0.nb)
            self.bank_wait(g1.yb)
            self.bank_wait(g1.nb)
            ready = {0: None, 1: None}
            while True:
                live = False
                for gg in (g0, g1):
                    if gg.st < gg.S:
                        live = True
                        emit_qk(gg)
                    if ready[gg.sub] is not None:
                        saved, gg.pend = gg.pend, ready[gg.sub]
                        emit_yn(gg)
                        gg.pend = saved
                    ready[gg.sub] = gg.pend
                    gg.pend = None
                if not live and ready[0] is None and ready[1] is None:
                    break
            pending = [(g0.hq, g0.tcol0, g0.yb, g0.nb, spe.n),
                       (g1.hq, g1.tcol0, g1.yb, g1.nb, spe.n)]
            self.prep_pump(5)
        for p in pending:
            fin_group(p)
        pending = []
        self.bfst_slab1_guard = (spe, spe.n)
        y_dve = sdve.n
        if self.debug:
            self.W(sy, sdve, y_dve)
            st = sy.dma_start(self.d_y[b], self.qt)
            self.s_out.inc(st, 16)

        self.W(sy, spe, spe.n)
        self.W(sy, self.s_cast, self.cast_marks["wc"])
        wcr = self.wc_b.rearrange("(ck p) j -> p ck j", p=128)
        for part in range(4):
            ld = sy.dma_start(self.wc_sb[:, 4 * part:4 * part + 4, :],
                              wcr[:, 4 * part:4 * part + 4, :])
            self.s_wc.inc(ld, 16)
        wc_ready = self.s_wc.n

        xres_guard = [[(spe, spe.n)], [(spe, spe.n)]]

        def fin_tile(tt):
            xr = self.xres[tt % 2]
            gcol = (tt % 2) * 8
            s1 = self.s1c[:, gcol:gcol + 8]
            vec = self.vec
            self.dve_edge(dv.scalar_tensor_tensor(vec[:, 5:6], s1[:, 0:1], 1.0, s1[:, 1:2],
                                                  ALU.bypass, ALU.add))
            self.dve_edge(dv.scalar_tensor_tensor(vec[:, 6:7], s1[:, 2:3], 1.0, s1[:, 3:4],
                                                  ALU.bypass, ALU.add))
            self.dve_edge(dv.tensor_tensor(vec[:, 0:1], vec[:, 5:6], vec[:, 6:7], ALU.add))
            self.dve_edge(dv.scalar_tensor_tensor(vec[:, 5:6], s1[:, 4:5], 1.0, s1[:, 5:6],
                                                  ALU.bypass, ALU.add))
            self.dve_edge(dv.scalar_tensor_tensor(vec[:, 6:7], s1[:, 6:7], 1.0, s1[:, 7:8],
                                                  ALU.bypass, ALU.add))
            self.dve_edge(dv.tensor_tensor(vec[:, 1:2], vec[:, 5:6], vec[:, 6:7], ALU.add))
            self.dve_edge(dv.tensor_scalar(vec[:, 0:1], vec[:, 0:1], 1.0 / 2048.0, None,
                                           ALU.mult))
            self.dve_edge(dv.tensor_scalar(vec[:, 1:2], vec[:, 1:2], 1.0 / 2048.0, None,
                                           ALU.mult))
            self.dve_edge(dv.scalar_tensor_tensor(vec[:, 2:3], vec[:, 0:1], 1.0, vec[:, 0:1],
                                                  ALU.bypass, ALU.mult))
            si = dv.tensor_tensor(vec[:, 1:2], vec[:, 1:2], vec[:, 2:3], ALU.subtract)
            sv = sdve.inc(si, 1)
            self.W(ac, sdve, sv)
            a0 = ac.activation(vec[:, 3:4], vec[:, 1:2], AF.Ln, bias=self.biases[:, 0:1],
                               scale=1.0)
            v0 = sact.inc(a0, 1)
            self.W(ac, sact, v0)
            ai = ac.activation(vec[:, 3:4], vec[:, 3:4], AF.Exp, scale=-0.5)
            av = sact.inc(ai, 1)
            self.W(dv, sact, av)
            ni = dv.tensor_scalar(xr, xr, vec[:, 0:1], vec[:, 3:4],
                                  ALU.subtract, ALU.mult)
            nv = sdve.inc(ni, 1)
            if self.debug and b == 0:
                self.W(sy, sdve, nv)
                dst = sy.dma_start(self.d_vec[tt], self.vec)
                self.s_out.inc(dst, 16)
                dst = sy.dma_start(self.d_s1c[tt], self.s1c[:, gcol:gcol + 8])
                sov = self.s_out.inc(dst, 16)
                self.W(dv, self.s_out, sov)
            self.W(g, sdve, nv)
            st1 = g.dma_start(self.xln1[b, tt * 128:(tt + 1) * 128, :], xr)
            st1v = self.s_sc.inc(st1, 16)
            self.W(dv, self.s_sc, self.bfst_guard)
            self.Wg(dv, self.bfst_slab1_guard)
            self.bfst_slab1_guard = None
            ci = dv.tensor_copy(self.bfst, xr)
            cv = sdve.inc(ci, 1)
            self.W(g, sdve, cv)
            st2 = g.dma_start(self.xln1b[b, tt * 128:(tt + 1) * 128, :], self.bfst)
            self.s_sc.inc(st2, 16)
            self.bfst_guard = self.s_sc.n
            xres_guard[tt % 2] = [(self.s_sc, st1v), (sdve, cv)]

        for tt in range(16):
            xr = self.xres[tt % 2]
            gcol = (tt % 2) * 8
            for gd in (xres_guard[tt % 2] or []):
                self.Wg(g, gd)
            ldr = g.dma_start(xr, self.xin[b, tt * 128:(tt + 1) * 128, :])
            sxr = self.s_xr[tt % 2]
            xr_ready = sxr.inc(ldr, 16)
            self.W(dv, sxr, xr_ready)
            self.W(dv, sdve, self.mark_bcbc)
            dv.memset(self.s1c[:, gcol:gcol + 8], 0.0)
            dv.tensor_tensor(xr, xr, self.bc_bc, ALU.add)
            for jr in range(4):
                bk = jr
                self.bank_wait(bk)
                self.W(te, self.s_wc, wc_ready)
                self.W(te, sdve, y_dve)
                for ck in range(16):
                    mm = te.matmul(self.bank[bk], self.qt[:, ck, tt * 128:(tt + 1) * 128],
                                   self.wc_sb[:, ck, jr * 512:(jr + 1) * 512],
                                   start=(ck == 0), stop=(ck == 15))
                pv = spe.inc(mm, 1)
                self.W(dv, spe, pv)
                ev = dv.scalar_tensor_tensor(xr[:, jr * 512:(jr + 1) * 512], self.bank[bk],
                                             1.0, xr[:, jr * 512:(jr + 1) * 512],
                                             ALU.bypass, ALU.add,
                                             accum_out=self.s1c[:, gcol + jr:gcol + jr + 1])
                dvv = sdve.inc(ev, 1)
                self.bank_done(bk, sdve, dvv)
            for hh in range(4):
                dv.scalar_tensor_tensor(self.epen[hh % 2], xr[:, hh * 512:(hh + 1) * 512],
                                        1.0, xr[:, hh * 512:(hh + 1) * 512],
                                        ALU.bypass, ALU.mult,
                                        accum_out=self.s1c[:, gcol + 4 + hh:gcol + 5 + hh])
            if tt >= 1:
                fin_tile(tt - 1)
            self.prep_pump(2)
        fin_tile(15)
        self.slab_guard[0] = (self.s_sc, self.s_sc.n)
        self.slab_guard[1] = (self.s_sc, self.s_sc.n)
        g4 = [(self.s_sc, self.s_sc.n), (self.s_dve, self.s_dve.n)]
        self.xTq_guard[0] = g4
        self.xTq_guard[1] = g4
        self.a4_done = {"pe": spe.n, "dve": sdve.n, "sc": self.s_sc.n, "act": sact.n}
        self.sc_mark[b] = self.s_sc.n

    def emit_F(self):
        nc = self.nc
        te, dv, ac, sy, g = nc.tensor, nc.vector, nc.scalar, nc.sync, nc.gpsimd
        spe, sdve, sact, sld, sgp = self.s_pe, self.s_dve, self.s_act, self.s_ld, self.s_gp

        self.prep_pump(10 ** 9)

        self.W(te, sdve, self.a4_done["dve"])
        for jr in range(4):
            self.W(g, spe, spe.n if jr > 0 else self.a4_done["pe"])
            g.dma_start(self.bcast_row(jr), self.wsdp[:, jr * 512:(jr + 1) * 512])\
                .then_inc(sgp.h, 16)
            sgp.n += 16
            bk = 6 + (jr % 2)
            self.bank_wait(bk)
            self.W(te, sgp, sgp.n)
            mm = te.matmul(self.bank[bk], self.ones_r[0:1, :], self.bcast_row(jr),
                           start=True, stop=True)
            pv = spe.inc(mm, 1)
            self.W(dv, spe, pv)
            cp = dv.tensor_scalar(self.wsdp_bc[:, jr * 512:(jr + 1) * 512], self.bank[bk],
                                  64.0, None, ALU.mult)
            dvv = sdve.inc(cp, 1)
            self.bank_done(bk, sdve, dvv)
        wsdp_ready = sdve.n
        if self.debug:
            self.W(sy, sdve, wsdp_ready)
            dst = sy.dma_start(self.d_wsbc, self.wsdp_bc)
            self.s_out.inc(dst, 16)

        wupr = self.wupT.rearrange("(ck p) i -> p ck i", p=128)
        wgr = self.wgateT.rearrange("(ck p) i -> p ck i", p=128)

        a4pe = self.a4_done["pe"]
        uslab_guard = [(spe, a4pe), (spe, a4pe)]
        gslab_guard = [(spe, a4pe), (spe, a4pe)]
        wd_guard = [(sdve, self.a4_done["dve"])] * 3
        xlres_guard = [None, None]
        hs2_guard = [None, None]
        ups_guard = [None, None]
        xout_guard = [(self.s_sc, self.a4_done["sc"])]
        x2_guard = None
        x3_guard = None
        btmp_first = True

        for c in range(8):
            b = c // 4
            trange = (c % 4) * 512
            if c == 0:
                self.W(sy, self.s_sc, self.sc_mark[b])
                self.W(sy, spe, a4pe)
                for gidx in range(16):
                    ld = sy.dma_start(self.xlnT[:, gidx, :],
                                      self.xln1b[b, trange:trange + 512,
                                                 gidx * 128:(gidx + 1) * 128],
                                      transpose=True)
                    sld.inc(ld, 16)
                self.xlnT_ready = sld.n
            xlnT_ready = self.xlnT_ready

            for gidx in range(16):
                hsb = self.hs2[gidx % 2]
                if c == 0 or gidx >= 2:
                    self.Wg(dv, hs2_guard[gidx % 2])
                    hi = dv.tensor_scalar(hsb, self.h128f, self.suug_c[:, gidx:gidx + 1],
                                          C2, ALU.mult, ALU.mult)
                    hv = sdve.inc(hi, 1)
                else:
                    hv = self.hs2_pre[gidx]
                bk = 4 + (gidx % 2)
                self.bank_wait(bk)
                self.W(te, sdve, hv)
                self.W(te, sld, xlnT_ready)
                if x2_guard is not None:
                    self.Wg(te, x2_guard)
                mm = te.matmul(self.bank[bk], hsb, self.xlnT[:, gidx, :], start=True, stop=True)
                pv = spe.inc(mm, 1)
                hs2_guard[gidx % 2] = (spe, pv)
                self.W(dv, spe, pv)
                ev = dv.tensor_copy(self.x2[:, gidx, :], self.bank[bk])
                dvv = sdve.inc(ev, 1)
                self.bank_done(bk, sdve, dvv)
            if btmp_first:
                self.W(dv, self.s_sc, self.a4_done["sc"])
                btmp_first = False
            kk = 1
            while kk < 16:
                r = 0
                while r < 16:
                    for j in range(kk):
                        a1 = self.x2[:, r + j, :]
                        b1 = self.x2[:, r + j + kk, :]
                        dv.tensor_copy(self.btmp, a1)
                        dv.tensor_tensor(a1, self.btmp, b1, ALU.add)
                        dv.tensor_tensor(b1, self.btmp, b1, ALU.subtract)
                    r += 2 * kk
                kk *= 2
            ii = dv.engine_nop()
            x2_dv = sdve.inc(ii, 1)
            self.f2_pe_done = spe.n

            bf_done = [0] * 64
            bf_prod = [False] * 64
            bf_comp = [None] * 64
            bf_order = []
            bf_last = [-10] * 64
            bf_ctr = [0]
            BF_DIST = 6

            def bf_pump(final=False):
                while True:
                    progress = True
                    stuck = False
                    while progress:
                        progress = False
                        for s in range(6):
                            step = 1 << s
                            for i0 in range(0, 64, 2 * step):
                                for j in range(step):
                                    i = i0 + j
                                    jj = i + step
                                    if not (bf_prod[i] and bf_prod[jj]
                                            and bf_done[i] == s and bf_done[jj] == s):
                                        continue
                                    if max(bf_last[i], bf_last[jj]) > bf_ctr[0] - BF_DIST:
                                        stuck = True
                                        continue
                                    a1 = self.x3[:, i, :]
                                    b1 = self.x3[:, jj, :]
                                    dv.tensor_copy(self.btmp, a1)
                                    dv.tensor_tensor(a1, self.btmp, b1, ALU.add)
                                    ii2 = dv.tensor_tensor(b1, self.btmp, b1, ALU.subtract)
                                    bf_ctr[0] += 3
                                    bf_done[i] += 1
                                    bf_done[jj] += 1
                                    bf_last[i] = bf_last[jj] = bf_ctr[0]
                                    progress = True
                                    if final:
                                        bf_last_instr[0] = ii2
                                    if bf_done[i] == 6:
                                        mk = sdve.inc(ii2, 1)
                                        bf_comp[i] = mk
                                        bf_comp[jj] = mk
                                        bf_order.append(i)
                                        bf_order.append(jj)
                    if final and stuck:
                        self.dve_edge(bf_last_instr[0])
                        for i in range(64):
                            bf_last[i] = -10
                        continue
                    return

            bf_last_instr = [None]

            for ig in range(32):
                us = self.uslab[ig % 2]
                gs = self.gslab[ig % 2]
                sus = self.s_us[ig % 2]
                sgs = self.s_gs[ig % 2]
                self.W(sy, self.s_tp, self.prep_done())
                self.Wg(sy, uslab_guard[ig % 2])
                ld = sy.dma_start(us, wupr[:, :, ig * 256:(ig + 1) * 256])
                sus.inc(ld, 16)
                self.Wg(sy, gslab_guard[ig % 2])
                ld = sy.dma_start(gs, wgr[:, :, ig * 256:(ig + 1) * 256])
                sgs.inc(ld, 16)
                for il in range(2):
                    it = ig * 2 + il
                    ub = il
                    gb = 2 + il
                    self.bank_wait(ub)
                    self.bank_wait(gb)
                    self.W(te, sus, sus.n)
                    self.W(te, sgs, sgs.n)
                    self.W(te, sdve, x2_dv)
                    if x3_guard is not None:
                        self.Wg(te, x3_guard)
                    for ck in range(16):
                        mm = te.matmul(self.bank[ub], us[:, ck, il * 128:(il + 1) * 128],
                                       self.x2[:, ck, :], start=(ck == 0), stop=(ck == 15))
                    upv = spe.inc(mm, 1)
                    for ck in range(16):
                        mm = te.matmul(self.bank[gb], gs[:, ck, il * 128:(il + 1) * 128],
                                       self.x2[:, ck, :], start=(ck == 0), stop=(ck == 15))
                    gpv = spe.inc(mm, 1)
                    if il == 1:
                        uslab_guard[ig % 2] = (spe, upv)
                        gslab_guard[ig % 2] = (spe, gpv)
                    upslot = self.ups[it % 2]
                    self.W(dv, spe, upv)
                    self.Wg(dv, ups_guard[it % 2])
                    ui = dv.tensor_scalar(upslot, self.bank[ub], self.wsu_up[:, it:it + 1],
                                          None, ALU.mult)
                    uv = sdve.inc(ui, 1)
                    self.bank_done(ub, sdve, uv)
                    self.W(ac, spe, gpv)
                    ai = ac.activation(self.x3[:, it, :], self.bank[gb], AF.Silu,
                                       scale=self.wsu_gt[:, it:it + 1])
                    av = sact.inc(ai, 1)
                    self.bank_done(gb, sact, av)
                    self.W(dv, sact, av)
                    xi = dv.scalar_tensor_tensor(self.x3[:, it, :], self.x3[:, it, :],
                                                 self.sudp_c[:, it:it + 1], upslot,
                                                 ALU.mult, ALU.mult)
                    xv = sdve.inc(xi, 1)
                    ups_guard[it % 2] = (sdve, xv)
                    bf_prod[it] = True
                    bf_last[it] = bf_ctr[0]
                    bf_ctr[0] += 1
                    bf_pump()

            bf_last_instr[0] = dv.engine_nop()
            bf_pump(final=True)
            assert all(d == 6 for d in bf_done), bf_done
            ii = dv.engine_nop()
            x3_dv = sdve.inc(ii, 1)
            if self.debug and c == 0:
                self.W(sy, sdve, x2_dv)
                st = sy.dma_start(self.d_x2, self.x2)
                self.s_out.inc(st, 16)
                self.W(sy, sdve, x3_dv)
                st = sy.dma_start(self.d_x3, self.x3)
                self.s_out.inc(st, 16)

            if c + 1 < 8:
                nb2 = (c + 1) // 4
                ntr = ((c + 1) % 4) * 512
                self.W(sy, self.s_sc, self.sc_mark[nb2])
                self.W(sy, spe, self.f2_pe_done)
                for gidx in range(16):
                    ld = sy.dma_start(self.xlnT[:, gidx, :],
                                      self.xln1b[nb2, ntr:ntr + 512,
                                                 gidx * 128:(gidx + 1) * 128],
                                      transpose=True)
                    sld.inc(ld, 16)
                self.xlnT_ready = sld.n

            if self.debug and c == 0:
                self.W(dv, self.s_out, self.s_out.n)
            assert len(bf_order) == 64
            x4_mark = [None] * 64
            for oi, itt in enumerate(bf_order):
                bk = 4 + (oi % 2)
                self.bank_wait(bk)
                self.W(te, sdve, bf_comp[itt])
                mm = te.matmul(self.bank[bk], self.h128s, self.x3[:, itt, :],
                               start=True, stop=True)
                pv = spe.inc(mm, 1)
                self.W(dv, spe, pv)
                ev = dv.tensor_copy(self.x3[:, itt, :], self.bank[bk])
                dvv = sdve.inc(ev, 1)
                x4_mark[itt] = dvv
                self.bank_done(bk, sdve, dvv)

            for jrh in range(2):
                for ti, tau in enumerate(bf_order):
                    ws = self.wdsl[ti % 3]
                    swd = self.s_wd[ti % 3]
                    self.W(sy, self.s_tp, self.prep_done())
                    self.Wg(sy, wd_guard[ti % 3])
                    ld = sy.dma_start(ws[:, 0:1024], self.wdT[tau * 128:(tau + 1) * 128,
                                                               jrh * 1024:(jrh + 1) * 1024])
                    swd.inc(ld, 16)
                    self.W(te, swd, swd.n)
                    self.W(te, sdve, x4_mark[tau])
                    for tt in range(4):
                        for jrl in range(2):
                            if ti == 0:
                                self.bank_wait(tt * 2 + jrl)
                            mm = te.matmul(self.bank[tt * 2 + jrl],
                                           self.x3[:, tau, tt * 128:(tt + 1) * 128],
                                           ws[:, jrl * 512:(jrl + 1) * 512],
                                           start=(ti == 0), stop=(ti == 63))
                    pv = spe.inc(mm, 1)
                    wd_guard[ti % 3] = (spe, pv)
                dn_pe = spe.n
                for tt in range(4):
                    xlr = self.xlres[tt % 2]
                    self.Wg(g, xlres_guard[tt % 2])
                    ldr = g.dma_start(xlr, self.xln1[b, trange + tt * 128:
                                                     trange + (tt + 1) * 128,
                                                     jrh * 1024:(jrh + 1) * 1024])
                    sxl = self.s_xl[tt % 2]
                    rv = sxl.inc(ldr, 16)
                    for jrl in range(2):
                        jcol = jrh * 1024 + jrl * 512
                        bk = tt * 2 + jrl
                        self.W(dv, spe, dn_pe)
                        self.W(dv, sdve, wsdp_ready)
                        for gd in xout_guard:
                            self.Wg(dv, gd)
                        dv.tensor_tensor(self.xout[:, tt, jcol:jcol + 512], self.bank[bk],
                                         self.wsdp_bc[:, jcol:jcol + 512], ALU.mult)
                        self.W(dv, sxl, rv)
                        ei = dv.scalar_tensor_tensor(self.xout[:, tt, jcol:jcol + 512],
                                                     self.xout[:, tt, jcol:jcol + 512], 1.0,
                                                     xlr[:, jrl * 512:(jrl + 1) * 512],
                                                     ALU.bypass, ALU.add)
                        dvv = sdve.inc(ei, 1)
                        self.bank_done(bk, sdve, dvv)
                    xlres_guard[tt % 2] = (sdve, sdve.n)
            xout_guard = []

            vec = self.vec
            for tt in range(4):
                xo = self.xout[:, tt, :]
                gcol = tt * 8
                dv.memset(self.s1c[:, gcol:gcol + 8], 0.0)
                for hh in range(4):
                    dv.scalar_tensor_tensor(self.f6tmp, xo[:, hh * 512:(hh + 1) * 512], 1.0,
                                            xo[:, hh * 512:(hh + 1) * 512],
                                            ALU.bypass, ALU.bypass,
                                            accum_out=self.s1c[:, gcol + hh:gcol + hh + 1])
                    dv.scalar_tensor_tensor(self.f6tmp, xo[:, hh * 512:(hh + 1) * 512],
                                            1.0, xo[:, hh * 512:(hh + 1) * 512],
                                            ALU.bypass, ALU.mult,
                                            accum_out=self.s1c[:, gcol + 4 + hh:gcol + 5 + hh])
            for tt in range(4):
                xo = self.xout[:, tt, :]
                s1 = self.s1c[:, tt * 8:tt * 8 + 8]
                self.dve_edge(dv.scalar_tensor_tensor(vec[:, 5:6], s1[:, 0:1], 1.0,
                                                      s1[:, 1:2], ALU.bypass, ALU.add))
                self.dve_edge(dv.scalar_tensor_tensor(vec[:, 6:7], s1[:, 2:3], 1.0,
                                                      s1[:, 3:4], ALU.bypass, ALU.add))
                self.dve_edge(dv.tensor_tensor(vec[:, 0:1], vec[:, 5:6], vec[:, 6:7],
                                               ALU.add))
                self.dve_edge(dv.scalar_tensor_tensor(vec[:, 5:6], s1[:, 4:5], 1.0,
                                                      s1[:, 5:6], ALU.bypass, ALU.add))
                self.dve_edge(dv.scalar_tensor_tensor(vec[:, 6:7], s1[:, 6:7], 1.0,
                                                      s1[:, 7:8], ALU.bypass, ALU.add))
                self.dve_edge(dv.tensor_tensor(vec[:, 1:2], vec[:, 5:6], vec[:, 6:7],
                                               ALU.add))
                self.dve_edge(dv.tensor_scalar(vec[:, 0:1], vec[:, 0:1], 1.0 / 2048.0,
                                               None, ALU.mult))
                self.dve_edge(dv.tensor_scalar(vec[:, 1:2], vec[:, 1:2], 1.0 / 2048.0,
                                               None, ALU.mult))
                self.dve_edge(dv.scalar_tensor_tensor(vec[:, 2:3], vec[:, 0:1], 1.0,
                                                      vec[:, 0:1], ALU.bypass, ALU.mult))
                si = dv.tensor_tensor(vec[:, 1:2], vec[:, 1:2], vec[:, 2:3], ALU.subtract)
                sv = sdve.inc(si, 1)
                self.W(ac, sdve, sv)
                a0 = ac.activation(vec[:, 3:4], vec[:, 1:2], AF.Ln, bias=self.biases[:, 0:1],
                                   scale=1.0)
                v0 = sact.inc(a0, 1)
                self.W(ac, sact, v0)
                ai = ac.activation(vec[:, 3:4], vec[:, 3:4], AF.Exp, scale=-0.5)
                av = sact.inc(ai, 1)
                self.W(dv, sact, av)
                ni = dv.tensor_scalar(xo, xo, vec[:, 0:1], vec[:, 3:4],
                                      ALU.subtract, ALU.mult)
                nv = sdve.inc(ni, 1)
                self.W(g, sdve, nv)
                st = g.dma_start(self.out[b, trange + tt * 128:trange + (tt + 1) * 128, :], xo)
                self.s_out.inc(st, 16)
            xout_guard = [(self.s_out, self.s_out.n)]
            x2_guard = (spe, spe.n)
            x3_guard = (spe, spe.n)
            if c + 1 < 8:
                self.hs2_pre = {}
                for gidx in range(2):
                    self.Wg(dv, hs2_guard[gidx % 2])
                    hi = dv.tensor_scalar(self.hs2[gidx % 2], self.h128f,
                                          self.suug_c[:, gidx:gidx + 1], C2,
                                          ALU.mult, ALU.mult)
                    self.hs2_pre[gidx] = sdve.inc(hi, 1)

    def build(self):
        self.emit_init()
        for b in range(NB):
            self.emit_A(b)
        self.emit_F()
        self.nc.sync.wait_ge(self.s_out.h, self.s_out.n)
        return self.nc


def _consts():
    h = np.array([[1.0]], dtype=np.float32)
    for _ in range(7):
        h = np.block([[h, h], [h, -h]])
    i = np.arange(128)
    tri = (i[:, None] <= i[None, :]).astype(ml_dtypes.bfloat16)
    eye8 = np.eye(8, dtype=np.float32)
    onesrow = np.ones((1, 128), dtype=np.float32)
    return h.astype(np.float32), tri, eye8, onesrow


def kernel(L0_attn_hidden_states, L0_attn_Wq, L0_attn_Wk, L0_attn_Wv, L0_attn_Wg,
           L0_attn_Wc, L0_attn_bc, L0_ffn_SU_ug, L0_ffn_SU_dp, L0_ffn_Wscale_ug,
           L0_ffn_Wscale_dp, L0_ffn_W_up, L0_ffn_W_gate, L0_ffn_W_down):
    h128, tri, eye8, onesrow = _consts()
    bld = B(debug=False)
    nc = bld.build()
    x = np.ascontiguousarray(np.asarray(L0_attn_hidden_states, np.float32))
    cnp = lambda a: np.ascontiguousarray(np.asarray(a, np.float32))
    shared = dict(
        wq=cnp(L0_attn_Wq), wk=cnp(L0_attn_Wk), wv=cnp(L0_attn_Wv), wgi=cnp(L0_attn_Wg),
        wc=cnp(L0_attn_Wc), bc=cnp(L0_attn_bc).reshape(1, C),
        su_ug=cnp(L0_ffn_SU_ug).reshape(1, C), su_dp=cnp(L0_ffn_SU_dp).reshape(1, INTER),
        wsug=cnp(L0_ffn_Wscale_ug).reshape(1, 2 * INTER),
        wsdp=cnp(L0_ffn_Wscale_dp).reshape(1, C),
        wup=cnp(L0_ffn_W_up), wgate=cnp(L0_ffn_W_gate), wdown=cnp(L0_ffn_W_down),
        h128=h128, tri=tri, eye8=eye8, onesrow=onesrow,
    )
    in_maps = []
    for cidx in range(8):
        mm = dict(shared)
        mm["xin"] = np.ascontiguousarray(x[2 * cidx:2 * cidx + 2])
        in_maps.append(mm)
    res = run_bass_kernel_spmd(nc, in_maps, core_ids=list(range(8)))
    return np.concatenate([np.asarray(r["out"], np.float32) for r in res.results], axis=0)


# version 16
# speedup vs baseline: 1.0644x; 1.0637x over previous
import math
import numpy as np
import ml_dtypes
import concourse.bass as bass
import concourse.mybir as mybir
from concourse.bass_utils import run_bass_kernel_spmd

F32 = mybir.dt.float32
BF16 = mybir.dt.bfloat16
AF = mybir.ActivationFunctionType
ALU = mybir.AluOpType
AX = mybir.AxisListType

T = 2048
C = 2048
NB = 2
H = 8
HQ = 16
INTER = 8192
GATE_OFFSET = 6.906768
C2 = 1.0 / (64.0 * math.sqrt(2048.0))
CD = 1.0 / (64.0 * math.sqrt(8192.0))
BIAS_EP = -0.25 * math.log(128.0)
SBUF_TOP = 229344


class Sem:
    def __init__(self, nc, name):
        self.h = nc.semaphore(name).__enter__()
        self.n = 0

    def inc(self, instr, k):
        instr.then_inc(self.h, k)
        self.n += k
        return self.n


class B:
    def __init__(self, debug=False):
        self.debug = debug
        nc = bass.Bass("TRN2", target_bir_lowering=False)
        self.nc = nc
        self.wm = {}

        dt = nc.dram_tensor
        self.xin = dt("xin", [NB, T, C], F32, kind="ExternalInput").ap()
        self.wq = dt("wq", [C, C], F32, kind="ExternalInput").ap()
        self.wk = dt("wk", [C, 1024], F32, kind="ExternalInput").ap()
        self.wv = dt("wv", [C, 1024], F32, kind="ExternalInput").ap()
        self.wgi = dt("wgi", [C, H], F32, kind="ExternalInput").ap()
        self.wc = dt("wc", [C, C], F32, kind="ExternalInput").ap()
        self.bc = dt("bc", [1, C], F32, kind="ExternalInput").ap()
        self.su_ug = dt("su_ug", [1, C], F32, kind="ExternalInput").ap()
        self.su_dp = dt("su_dp", [1, INTER], F32, kind="ExternalInput").ap()
        self.wsug = dt("wsug", [1, 2 * INTER], F32, kind="ExternalInput").ap()
        self.wsdp = dt("wsdp", [1, C], F32, kind="ExternalInput").ap()
        self.wup = dt("wup", [INTER, C], F32, kind="ExternalInput").ap()
        self.wgate = dt("wgate", [INTER, C], F32, kind="ExternalInput").ap()
        self.wdown = dt("wdown", [C, INTER], F32, kind="ExternalInput").ap()
        self.h128 = dt("h128", [128, 128], F32, kind="ExternalInput").ap()
        self.tri = dt("tri", [128, 128], BF16, kind="ExternalInput").ap()
        self.eye8 = dt("eye8", [8, 8], F32, kind="ExternalInput").ap()
        self.onesrow = dt("onesrow", [1, 128], F32, kind="ExternalInput").ap()

        self.out = dt("out", [NB, T, C], F32, kind="ExternalOutput").ap()

        self.xbf = dt("xbf", [NB, T, C], BF16, kind="Internal").ap()
        self.wq_b = dt("wq_b", [C, C], BF16, kind="Internal").ap()
        self.wk_b = dt("wk_b", [C, 1024], BF16, kind="Internal").ap()
        self.wv_b = dt("wv_b", [C, 1024], BF16, kind="Internal").ap()
        self.wg_b = dt("wg_b", [C, H], BF16, kind="Internal").ap()
        self.wc_b = dt("wc_b", [C, C], BF16, kind="Internal").ap()
        self.wup_b = dt("wup_b", [INTER, C], BF16, kind="Internal").ap()
        self.wgate_b = dt("wgate_b", [INTER, C], BF16, kind="Internal").ap()
        self.wdown_b = dt("wdown_b", [C, INTER], BF16, kind="Internal").ap()
        self.wupT = dt("wupT", [C, INTER], BF16, kind="Internal").ap()
        self.wgateT = dt("wgateT", [C, INTER], BF16, kind="Internal").ap()
        self.wdT = dt("wdT", [INTER, C], BF16, kind="Internal").ap()
        xk = "ExternalOutput" if debug else "Internal"
        self.xln1 = dt("xln1", [NB, T, C], F32, kind=xk).ap()
        self.xln1b = dt("xln1b", [NB, T, C], BF16, kind="Internal").ap()
        if debug:
            self.d_qt = dt("d_qt", [NB, 128, HQ, T], BF16, kind="ExternalOutput").ap()
            self.d_kt = dt("d_kt", [NB, 128, H, T], BF16, kind="ExternalOutput").ap()
            self.d_v = dt("d_v", [NB, 128, 16, 1024], BF16, kind="ExternalOutput").ap()
            self.d_y = dt("d_y", [NB, 128, HQ, T], BF16, kind="ExternalOutput").ap()
            self.d_x2 = dt("d_x2", [128, 16, 512], BF16, kind="ExternalOutput").ap()
            self.d_x3 = dt("d_x3", [128, 64, 512], BF16, kind="ExternalOutput").ap()
            self.d_vec = dt("d_vec", [16, 128, 8], F32, kind="ExternalOutput").ap()
            self.d_s1c = dt("d_s1c", [16, 128, 8], F32, kind="ExternalOutput").ap()
            self.d_wsbc = dt("d_wsbc", [128, C], F32, kind="ExternalOutput").ap()

        base = (SBUF_TOP - nc.sbuf_bytes_remaining + 31) & ~31
        o = {
            "big0": base,
            "big12": base + 65536,
            "big3": base + 131072,
            "sl": base + 163840,
            "wring": base + 180224,
            "bcr": base + 192512,
            "misc": base + 200704,
        }
        self.o = o

        def at(name, shape, dtp, off):
            return nc.alloc_sbuf_tensor_at(name, shape, dtp, offset=off).ap()

        self.qt = at("qt", [128, HQ, T], BF16, o["big0"])
        self.kt = at("kt", [128, H, T], BF16, o["big12"])
        self.v = at("v", [128, 16, 1024], BF16, o["big12"] + 32768)
        self.wc_sb = at("wc_sb", [128, 16, C], BF16, o["big12"])
        self.xTq2 = [at("xTq", [128, 16, 512], BF16, o["big3"]),
                     at("xTqb", [128, 16, 512], BF16, o["big3"] + 16384)]
        self.xres = [at(f"xres{i}", [128, C], F32, o["big3"] + 16384 + 8192 * i) for i in range(2)]
        self.slab = [at(f"slab{i}", [128, 16, 256], BF16, o["sl"] + 8192 * i) for i in range(2)]
        self.a_sb = at("a_sb", [128, 8, 512], BF16, o["sl"])
        self.bfst = at("bfst", [128, C], BF16, o["sl"] + 8192)
        self.stg = [at(f"stg{i}", [128, 1024], BF16, o["wring"] + 2048 * i) for i in range(2)]
        self.epen = [at(f"epen{i}", [128, 512], F32, o["wring"] + 4096 + 2048 * i) for i in range(4)]
        self.rb_sb = at("rb_sb", [128, 512], F32, o["wring"] + 4096)
        self.nrm_t = at("nrm_t", [128, 512], F32, o["wring"] + 6144)
        self.x3 = at("x3", [128, 64, 512], BF16, o["big0"])
        self.uslab = [at(f"uslab{i}", [128, 16, 256], BF16, o["big12"] + 8192 * i) for i in range(2)]
        self.gslab = [at(f"gslab{i}", [128, 16, 256], BF16, o["big12"] + 16384 + 8192 * i) for i in range(2)]
        self.xlnT = at("xlnT", [128, 16, 512], BF16, o["big12"] + 32768)
        self.x2 = at("x2", [128, 16, 512], BF16, o["big12"] + 49152)
        self.xout = at("xout", [128, 4, C], F32, o["big3"])
        self.xlres = [at(f"xlres{i}", [128, 1024], F32, o["sl"] + 4096 * i) for i in range(2)]
        self.btmp = at("btmp", [128, 512], BF16, o["sl"] + 8192)
        self.ups = [at(f"ups{i}", [128, 512], BF16, o["sl"] + 9216 + 1024 * i) for i in range(2)]
        self.f6tmp = at("f6tmp", [128, 512], F32, o["sl"] + 11264)
        self.wdsl = [at(f"wdsl{i}", [128, 2048], BF16, o["wring"] + 4096 * i) for i in range(3)]
        self.bc_bc = at("bc_bc", [128, C], F32, o["bcr"])
        self.wsdp_bc = at("wsdp_bc", [128, C], F32, o["bcr"])
        m = o["misc"]
        def mk(name, shape, dtp, nbytes):
            nonlocal m
            t = at(name, shape, dtp, m)
            m += nbytes
            return t
        self.eye8_sb = mk("eye8_sb", [8, 8], F32, 32)
        self.h128f = mk("h128f", [128, 128], F32, 512)
        self.h128s = mk("h128s", [128, 128], BF16, 256)
        self.tri_sb = mk("tri_sb", [128, 128], BF16, 256)
        self.suug_c = mk("suug_c", [128, 16], F32, 64)
        self.sudp_c = mk("sudp_c", [128, 64], F32, 256)
        self.wsu_up = mk("wsu_up", [128, 64], F32, 256)
        self.wsu_gt = mk("wsu_gt", [128, 64], F32, 256)
        self.wg_sb = mk("wg_sb", [128, 16, 8], BF16, 256)
        self.ones_r = mk("ones_r", [1, 128], F32, 512)
        self.onescol = mk("onescol", [128, 1], BF16, 32)
        self.biases = mk("biases", [128, 8], F32, 32)
        self.gbuf0 = mk("gbuf0", [128, 512], F32, 2048)
        self.gbuf1 = mk("gbuf1", [128, 512], F32, 2048)
        self.r0row = mk("r0row", [1, 512], F32, 2048)
        self.r0row2 = mk("r0row2", [1, 512], F32, 2048)
        self.hs2 = [mk(f"hs2_{i}", [128, 128], BF16, 256) for i in range(2)]
        self.s1c = mk("s1c", [128, 32], F32, 128)
        self.vec = mk("vec", [128, 8], F32, 32)
        assert m <= SBUF_TOP, f"sbuf overflow: {m} > {SBUF_TOP}"
        self.misc_end = m

        self.pp = [nc.alloc_psum_tensor(f"pp{i}", [128, 1024], F32).ap() for i in range(4)]
        self.bank = [self.pp[i // 2][:, (i % 2) * 512:(i % 2 + 1) * 512] for i in range(8)]
        self.bank_guard = [None] * 8

        self.s_ld = Sem(nc, "s_ld")
        self.s_tp = Sem(nc, "s_tp")
        self.s_gp = Sem(nc, "s_gp")
        self.s_cast = Sem(nc, "s_cast")
        self.s_sc = Sem(nc, "s_sc")
        self.s_out = Sem(nc, "s_out")
        self.s_sl = [Sem(nc, "s_sl0"), Sem(nc, "s_sl1")]
        self.s_us = [Sem(nc, "s_us0"), Sem(nc, "s_us1")]
        self.s_gs = [Sem(nc, "s_gs0"), Sem(nc, "s_gs1")]
        self.s_wd = [Sem(nc, "s_wd0"), Sem(nc, "s_wd1"), Sem(nc, "s_wd2")]
        self.s_xq = [Sem(nc, "s_xq0"), Sem(nc, "s_xq1")]
        self.s_xr = [Sem(nc, "s_xr0"), Sem(nc, "s_xr1")]
        self.s_xl = [Sem(nc, "s_xl0"), Sem(nc, "s_xl1")]
        self.s_wc = Sem(nc, "s_wc")
        self.s_pe = Sem(nc, "s_pe")
        self.s_dve = Sem(nc, "s_dve")
        self.s_act = Sem(nc, "s_act")

        self.prep_q = []
        self.prep_i = 0
        self.cast_marks = {}
        self.slab_guard = [None, None]
        self.xTq_guard = [None, None]
        self.sc_mark = [0, 0]
        self.bfst_guard = 0

    def bcast_row(self, jr):
        return self.gbuf0[0:1, :]

    def W(self, eng, sem, val):
        if val is None or val <= 0:
            return
        key = (id(eng), id(sem))
        if self.wm.get(key, 0) >= val:
            return
        eng.wait_ge(sem.h, val)
        self.wm[key] = val

    def Wg(self, eng, guard):
        if guard is not None:
            self.W(eng, guard[0], guard[1])

    def dve_edge(self, instr):
        v = self.s_dve.inc(instr, 1)
        self.W(self.nc.vector, self.s_dve, v)
        return v

    def bank_wait(self, i):
        self.Wg(self.nc.tensor, self.bank_guard[i])

    def bank_done(self, i, sem, val):
        self.bank_guard[i] = (sem, val)

    def emit_init(self):
        nc = self.nc
        g, dv, te = nc.gpsimd, nc.vector, nc.tensor
        scast, sgp = self.s_cast, self.s_gp

        def cast(dst, src, name, parts=1):
            n = src.shape[0]
            step = n // parts
            for i in range(parts):
                g.dma_start(dst[i * step:(i + 1) * step], src[i * step:(i + 1) * step])\
                    .then_inc(scast.h, 16)
                scast.n += 16
            self.cast_marks[name] = scast.n
            g.wait_ge(scast.h, scast.n)

        cast(self.xbf[0], self.xin[0], "x0", parts=2)
        cast(self.wg_b, self.wgi, "wg")
        cast(self.wq_b, self.wq, "wq", parts=2)
        cast(self.wk_b, self.wk, "wk")
        cast(self.wv_b, self.wv, "wv")
        cast(self.xbf[1], self.xin[1], "x1", parts=2)
        cast(self.wc_b, self.wc, "wc", parts=2)
        cast(self.wup_b, self.wup, "wup", parts=4)
        cast(self.wgate_b, self.wgate, "wgate", parts=4)
        cast(self.wdown_b, self.wdown, "wdown", parts=4)

        def gload(dst, src, slow=False):
            g.dma_start(dst, src, allow_slow_non_contiguous=slow).then_inc(sgp.h, 16)
            sgp.n += 16

        gload(self.eye8_sb, self.eye8)
        gload(self.h128f, self.h128)
        gload(self.tri_sb, self.tri)
        gload(self.ones_r, self.onesrow)
        gload(self.suug_c, self.su_ug.rearrange("o (a p) -> (o p) a", p=128), slow=True)
        gload(self.sudp_c, self.su_dp.rearrange("o (a p) -> (o p) a", p=128), slow=True)
        gload(self.wsu_up, self.wsug[:, 0:INTER].rearrange("o (a p) -> (o p) a", p=128), slow=True)
        gload(self.wsu_gt, self.wsug[:, INTER:].rearrange("o (a p) -> (o p) a", p=128), slow=True)
        self.mark_gp_small = sgp.n
        g.wait_ge(sgp.h, sgp.n)
        self.W(g, scast, self.cast_marks["wg"])
        gload(self.wg_sb, self.wg_b.rearrange("(ck p) h -> p ck h", p=128), slow=True)
        self.mark_wg = sgp.n
        g.wait_ge(sgp.h, sgp.n)

        dv.memset(self.biases[:, 0:1], 1e-5)
        dv.memset(self.biases[:, 1:2], 1.0)
        dv.memset(self.biases[:, 2:3], -GATE_OFFSET)
        dv.memset(self.biases[:, 3:4], BIAS_EP)
        dv.memset(self.onescol, 1.0)
        dv.memset(self.gbuf1[32:40, 0:1], 0.0)
        self.W(dv, sgp, self.mark_gp_small)
        dv.tensor_scalar(self.wsu_up, self.wsu_up, 64.0, None, ALU.mult)
        dv.tensor_scalar(self.wsu_gt, self.wsu_gt, 64.0, None, ALU.mult)
        ii = dv.tensor_scalar(self.h128s, self.h128f, CD, None, ALU.mult)
        self.s_dve.inc(ii, 1)
        self.mark_dve_init = self.s_dve.n

        self.W(te, sgp, self.mark_gp_small)
        for jr in range(4):
            self.W(g, self.s_pe, self.s_pe.n)
            gload(self.bcast_row(jr), self.bc[:, jr * 512:(jr + 1) * 512])
            bk = 6 + (jr % 2)
            self.bank_wait(bk)
            self.W(te, sgp, sgp.n)
            mm = te.matmul(self.bank[bk], self.ones_r[0:1, :], self.bcast_row(jr),
                           start=True, stop=True)
            pv = self.s_pe.inc(mm, 1)
            self.W(dv, self.s_pe, pv)
            cp = dv.tensor_copy(self.bc_bc[:, jr * 512:(jr + 1) * 512], self.bank[bk])
            dvv = self.s_dve.inc(cp, 1)
            self.bank_done(bk, self.s_dve, dvv)
        self.mark_bcbc = self.s_dve.n

        for (nat, castm, dst, rows, cols) in [
            (self.wup_b, "wup", self.wupT, INTER, C),
            (self.wgate_b, "wgate", self.wgateT, INTER, C),
            (self.wdown_b, "wdown", self.wdT, C, INTER),
        ]:
            for cc in range(cols // 128):
                for rc in range(rows // 1024):
                    self.prep_q.append((nat, castm, dst, cc, rc))

    def prep_pump(self, k):
        sy = self.nc.sync
        n = 0
        while n < k and self.prep_i < len(self.prep_q):
            i = self.prep_i
            nat, castm, dst, cc, rc = self.prep_q[i]
            buf = self.stg[i % 2]
            self.W(sy, self.s_cast, self.cast_marks[castm])
            if i >= 2:
                self.W(sy, self.s_tp, 16 * (2 * (i - 2) + 2))
            ld = sy.dma_start(buf, nat[rc * 1024:(rc + 1) * 1024, cc * 128:(cc + 1) * 128],
                              transpose=True)
            self.s_tp.inc(ld, 16)
            self.W(sy, self.s_tp, 16 * (2 * i + 1))
            st = sy.dma_start(dst[cc * 128:(cc + 1) * 128, rc * 1024:(rc + 1) * 1024], buf)
            self.s_tp.inc(st, 16)
            self.prep_i += 1
            n += 1

    def prep_done(self):
        return 16 * 2 * len(self.prep_q)

    def emit_A(self, b):
        nc = self.nc
        te, dv, ac, sy, g = nc.tensor, nc.vector, nc.scalar, nc.sync, nc.gpsimd
        spe, sdve, sact, sld, sgp = self.s_pe, self.s_dve, self.s_act, self.s_ld, self.s_gp

        xbf_b = self.xbf[b]
        gq = self.gbuf0[0:8, :]
        e1q = self.gbuf0[32:40, :]
        tqb = [self.gbuf0[64:72, :], self.gbuf1[0:8, :]]
        off8 = self.gbuf1[32:40, 0:1]

        self.W(dv, sdve, self.mark_dve_init)
        if b > 0:
            dv.memset(self.gbuf1[32:40, 0:1], 0.0)

        slab_guard = self.slab_guard
        epen_guard = getattr(self, "epen_guard_state", [None] * 4)
        self.epen_guard_state = epen_guard
        ep_idx = getattr(self, "ep_idx_state", 0)
        wqr = self.wq_b.rearrange("(ck p) j -> p ck j", p=128)
        wkr = self.wk_b.rearrange("(ck p) j -> p ck j", p=128)
        wvr = self.wv_b.rearrange("(ck p) j -> p ck j", p=128)

        for tq in range(4):
            t0 = tq * 512
            xTq = self.xTq2[tq % 2]
            self.W(sy, self.s_cast, self.cast_marks[f"x{b}"])
            for gd in (self.xTq_guard[tq % 2] or []):
                self.Wg(sy, gd)
            sxq = self.s_xq[tq % 2]
            for gidx in range(16):
                ld = sy.dma_start(xTq[:, gidx, :],
                                  xbf_b[t0:t0 + 512, gidx * 128:(gidx + 1) * 128],
                                  transpose=True)
                sxq.inc(ld, 16)
            xT_ready = (sxq, sxq.n)

            self.bank_wait(4)
            self.W(te, xT_ready[0], xT_ready[1])
            self.W(te, sgp, self.mark_wg)
            for ck in range(16):
                mm = te.matmul(self.bank[4][0:8, :], self.wg_sb[:, ck, :], xTq[:, ck, :],
                               start=(ck == 0), stop=(ck == 15))
            gpe = spe.inc(mm, 1)
            self.W(dv, spe, gpe)
            cp = dv.tensor_copy(gq, self.bank[4][0:8, :])
            gdv = sdve.inc(cp, 1)
            self.bank_done(4, sdve, gdv)
            self.W(ac, sdve, gdv)
            a0 = ac.activation(e1q, gq, AF.Exp, bias=self.biases[0:8, 2:3], scale=-1.0)
            v0 = sact.inc(a0, 1)
            self.W(ac, sact, v0)
            ai = ac.activation(tqb[0], e1q, AF.Ln, bias=self.biases[0:8, 1:2], scale=1.0)
            av = sact.inc(ai, 1)
            self.W(dv, sact, av)
            cur = 0
            k = 1
            while k < 512:
                src, dst2 = tqb[cur], tqb[1 - cur]
                dv.tensor_tensor(dst2[:, k:512], src[:, k:512], src[:, 0:512 - k], ALU.add)
                self.dve_edge(dv.tensor_copy(dst2[:, 0:k], src[:, 0:k]))
                cur = 1 - cur
                k *= 2
            L1cq = tqb[cur]
            assert cur == 1, "L1cq must land at base partition 0 for the eye8 matmul"
            self.dve_edge(dv.tensor_scalar(L1cq, L1cq, off8, None, ALU.add))
            ii = dv.tensor_copy(off8, L1cq[:, 511:512])
            cum_dv = sdve.inc(ii, 1)
            self.W(dv, sdve, cum_dv)

            for s in range(4):
                sb = self.slab[s % 2]
                ssl = self.s_sl[s % 2]
                self.W(sy, self.s_cast, self.cast_marks["wv"])
                self.Wg(sy, slab_guard[s % 2])
                ld = sy.dma_start(sb, wvr[:, :, s * 256:(s + 1) * 256])
                ssl.inc(ld, 16)
                self.W(te, ssl, ssl.n)
                for tt in range(4):
                    bk = tt % 2
                    self.bank_wait(bk)
                    for ck in range(16):
                        mm = te.matmul(self.bank[bk][:, 0:256],
                                       xTq[:, ck, tt * 128:(tt + 1) * 128],
                                       sb[:, ck, :], start=(ck == 0), stop=(ck == 15))
                    pv = spe.inc(mm, 1)
                    if tt == 3:
                        slab_guard[s % 2] = (spe, pv)
                    self.W(dv, spe, pv)
                    ev = dv.tensor_copy(self.v[:, tq * 4 + tt, s * 256:(s + 1) * 256],
                                        self.bank[bk][:, 0:256])
                    dvv = sdve.inc(ev, 1)
                    self.bank_done(bk, sdve, dvv)

            self.W(te, sdve, cum_dv)
            r0 = [self.r0row, self.r0row2]
            r0_guard = getattr(self, "r0_guard", [None, None])
            self.r0_guard = r0_guard
            self.bank_wait(5)
            mm = te.matmul(self.bank[5][0:1, :], self.eye8_sb[:, 0:1], L1cq,
                           start=True, stop=True)
            pv = spe.inc(mm, 1)
            self.W(dv, spe, pv)
            self.Wg(dv, r0_guard[0])
            cpy = dv.tensor_copy(r0[0], self.bank[5][0:1, :])
            dvv = sdve.inc(cpy, 1)
            self.bank_done(5, sdve, dvv)
            copy_dv = dvv
            for h in range(H):
                self.bank_wait(6)
                self.W(te, sdve, copy_dv)
                mm = te.matmul(self.bank[6], self.ones_r[0:1, :], r0[h % 2],
                               start=True, stop=True)
                pv = spe.inc(mm, 1)
                r0_guard[h % 2] = (spe, pv)
                iep = ep_idx % 4
                ien = (ep_idx + 1) % 4
                ep_idx += 2
                self.W(ac, spe, pv)
                self.Wg(ac, epen_guard[iep])
                ac.activation(self.epen[iep], self.bank[6], AF.Exp,
                              bias=self.biases[:, 3:4], scale=-0.5)
                self.Wg(ac, epen_guard[ien])
                ai = ac.activation(self.epen[ien], self.bank[6], AF.Exp,
                                   bias=self.biases[:, 3:4], scale=0.5)
                av = sact.inc(ai, 1)
                self.bank_done(6, sact, av)
                if h + 1 < H:
                    self.bank_wait(5)
                    mm = te.matmul(self.bank[5][0:1, :], self.eye8_sb[:, h + 1:h + 2],
                                   L1cq, start=True, stop=True)
                    pv = spe.inc(mm, 1)
                    self.W(dv, spe, pv)
                    self.Wg(dv, r0_guard[(h + 1) % 2])
                    cpy = dv.tensor_copy(r0[(h + 1) % 2], self.bank[5][0:1, :])
                    copy_dv = sdve.inc(cpy, 1)
                    self.bank_done(5, sdve, copy_dv)

                sb = self.slab[h % 2]
                ssl = self.s_sl[h % 2]
                self.W(sy, self.s_cast, self.cast_marks["wq"])
                self.Wg(sy, slab_guard[h % 2])
                ld = sy.dma_start(sb, wqr[:, :, h * 256:(h + 1) * 256])
                ssl.inc(ld, 16)
                self.W(te, ssl, ssl.n)
                for jl in range(2):
                    jt = 2 * h + jl
                    bk = jl
                    self.bank_wait(bk)
                    for ck in range(16):
                        mm = te.matmul(self.bank[bk], sb[:, ck, jl * 128:(jl + 1) * 128],
                                       xTq[:, ck, :], start=(ck == 0), stop=(ck == 15))
                    pv = spe.inc(mm, 1)
                    if jl == 1:
                        slab_guard[h % 2] = (spe, pv)
                    self.W(dv, spe, pv)
                    self.W(dv, sact, av)
                    ev = dv.tensor_tensor(self.qt[:, jt, t0:t0 + 512], self.bank[bk],
                                          self.epen[iep], ALU.mult)
                    dvv = sdve.inc(ev, 1)
                    self.bank_done(bk, sdve, dvv)
                sb2 = self.slab[(h + 1) % 2]
                ssl2 = self.s_sl[(h + 1) % 2]
                self.W(sy, self.s_cast, self.cast_marks["wk"])
                self.Wg(sy, slab_guard[(h + 1) % 2])
                ld = sy.dma_start(sb2[:, :, 0:128], wkr[:, :, h * 128:(h + 1) * 128])
                ssl2.inc(ld, 16)
                self.W(te, ssl2, ssl2.n)
                self.bank_wait(2)
                for ck in range(16):
                    mm = te.matmul(self.bank[2], sb2[:, ck, 0:128], xTq[:, ck, :],
                                   start=(ck == 0), stop=(ck == 15))
                pv = spe.inc(mm, 1)
                slab_guard[(h + 1) % 2] = (spe, pv)
                self.W(dv, spe, pv)
                ev = dv.tensor_tensor(self.kt[:, h, t0:t0 + 512], self.bank[2],
                                      self.epen[ien], ALU.mult)
                dvv = sdve.inc(ev, 1)
                self.bank_done(2, sdve, dvv)
                epen_guard[iep] = (sdve, sdve.n)
                epen_guard[ien] = (sdve, dvv)
            self.xTq_guard[tq % 2] = [(spe, spe.n)]

        self.ep_idx_state = ep_idx
        proj_dve = sdve.n
        if self.debug:
            self.W(sy, sdve, proj_dve)
            for dst, src in [(self.d_qt[b], self.qt), (self.d_kt[b], self.kt),
                             (self.d_v[b], self.v)]:
                st = sy.dma_start(dst, src)
                self.s_out.inc(st, 16)
            dbg_mark = self.s_out.n

        self.W(te, sdve, proj_dve)
        a_guard = [None] * 8
        ag0 = slab_guard[0][1] if slab_guard[0] else 0
        ag1 = slab_guard[1][1] if slab_guard[1] else 0
        for _i in range(4):
            a_guard[_i] = (spe, ag0)
            a_guard[4 + _i] = (spe, ag1)
        self.bfst_slab1_guard = (spe, ag1)
        qk_pair = 0
        first_y = True
        pending = []

        def fin_group(p):
            nonlocal first_y
            (phq, ptcol0, pyb, pnb, pypv) = p
            tmp = self.nrm_t[0:1, :]
            rec = self.rb_sb[0:1, :]
            self.W(dv, spe, pypv)
            self.dve_edge(dv.tensor_scalar(tmp, self.bank[pnb][0:1, :], 1e-6, None, ALU.add))
            ri = dv.reciprocal(rec, tmp)
            rdv = sdve.inc(ri, 1)
            self.W(te, sdve, rdv)
            mm = te.matmul(self.bank[pnb], self.ones_r[0:1, :], rec, start=True, stop=True)
            pv = spe.inc(mm, 1)
            self.W(dv, spe, pv)
            cpy = dv.tensor_copy(self.rb_sb, self.bank[pnb])
            dvv = self.dve_edge(cpy)
            self.bank_done(pnb, sdve, dvv)
            if first_y and self.debug:
                self.W(dv, self.s_out, dbg_mark)
            first_y = False
            ev = dv.tensor_tensor(self.qt[:, phq, ptcol0:ptcol0 + 512], self.bank[pyb],
                                  self.rb_sb, ALU.mult)
            dvv = sdve.inc(ev, 1)
            self.bank_done(pyb, sdve, dvv)

        class Grp:
            def __init__(g, hq, tr, sub):
                g.hq = hq
                g.h = hq // 2
                g.tr = tr
                g.tcol0 = tr * 512
                g.S = 4 * tr + 4
                g.sub = sub
                g.yb = 4 + sub
                g.nb = 6 + sub
                g.st = 0
                g.chunk_par = 0
                g.pend = None

        def emit_qk(g):
            fused = (g.st + 1 < 4 * g.tr)
            nst = 2 if fused else 1
            pq = qk_pair[0] % 2
            qk_pair[0] += 1
            pairp = self.pp[pq]
            los = []
            for u in range(nst):
                sti = g.st + u
                kk = sti - 4 * g.tr
                lo = 0 if kk < 0 else 128 * kk
                los.append(lo)
                self.bank_wait(2 * pq + u)
                mm = te.matmul(pairp[:, u * 512 + lo:(u + 1) * 512],
                               self.kt[:, g.h, sti * 128:(sti + 1) * 128],
                               self.qt[:, g.hq, g.tcol0 + lo:g.tcol0 + 512],
                               start=True, stop=True)
            qkv_ = spe.inc(mm, 1)
            slot = g.sub * 4 + g.chunk_par * 2
            g.chunk_par ^= 1
            self.W(ac, spe, qkv_)
            for u in range(nst):
                self.Wg(ac, a_guard[slot + u])
            if fused:
                ai = ac.activation(
                    self.a_sb[:, slot:slot + 2, :].rearrange("p a b -> p (a b)"),
                    pairp, AF.Square)
            else:
                lo = los[0]
                ai = ac.activation(self.a_sb[:, slot, lo:512], pairp[:, lo:512], AF.Square)
            av = sact.inc(ai, 1)
            for u in range(nst):
                self.bank_done(2 * pq + u, sact, av)
            if fused:
                g.pend = (slot, nst, los, g.st, "act", av)
            else:
                lo = los[0]
                self.W(dv, sact, av)
                mi = dv.tensor_tensor(self.a_sb[:, slot, lo:lo + 128],
                                      self.a_sb[:, slot, lo:lo + 128],
                                      self.tri_sb, ALU.mult)
                dvv = sdve.inc(mi, 1)
                g.pend = (slot, nst, los, g.st, "dve", dvv)
            g.st += nst

        def emit_yn(g):
            (slot, nst, los, st0, wk, wv) = g.pend
            g.pend = None
            if wk == "act":
                self.W(te, sact, wv)
            else:
                self.W(te, sdve, wv)
            for u in range(nst):
                sti = st0 + u
                lo = los[u]
                te.matmul(self.bank[g.yb][:, lo:512],
                          self.v[:, sti, g.h * 128:(g.h + 1) * 128],
                          self.a_sb[:, slot + u, lo:512],
                          start=(sti == 0), stop=(sti == g.S - 1))
                mm2 = te.matmul(self.bank[g.nb][0:1, lo:512], self.onescol,
                                self.a_sb[:, slot + u, lo:512],
                                start=(sti == 0), stop=(sti == g.S - 1))
                pv = spe.inc(mm2, 1)
                a_guard[slot + u] = (spe, pv)

        qk_pair = [0]
        pairs = [(hq, tr) for hq in range(HQ) for tr in range(4)]
        for gi in range(0, len(pairs), 2):
            g0 = Grp(*pairs[gi], 0)
            g1 = Grp(*pairs[gi + 1], 1)
            for p in pending:
                fin_group(p)
            pending = []
            self.bank_wait(g0.yb)
            self.bank_wait(g0.nb)
            self.bank_wait(g1.yb)
            self.bank_wait(g1.nb)
            ready = {0: None, 1: None}
            while True:
                live = False
                for gg in (g0, g1):
                    if gg.st < gg.S:
                        live = True
                        emit_qk(gg)
                    if ready[gg.sub] is not None:
                        saved, gg.pend = gg.pend, ready[gg.sub]
                        emit_yn(gg)
                        gg.pend = saved
                    ready[gg.sub] = gg.pend
                    gg.pend = None
                if not live and ready[0] is None and ready[1] is None:
                    break
            pending = [(g0.hq, g0.tcol0, g0.yb, g0.nb, spe.n),
                       (g1.hq, g1.tcol0, g1.yb, g1.nb, spe.n)]
            self.prep_pump(5)
        for p in pending:
            fin_group(p)
        pending = []
        self.bfst_slab1_guard = (spe, spe.n)
        y_dve = sdve.n
        if self.debug:
            self.W(sy, sdve, y_dve)
            st = sy.dma_start(self.d_y[b], self.qt)
            self.s_out.inc(st, 16)

        self.W(sy, spe, spe.n)
        self.W(sy, self.s_cast, self.cast_marks["wc"])
        wcr = self.wc_b.rearrange("(ck p) j -> p ck j", p=128)
        for part in range(4):
            ld = sy.dma_start(self.wc_sb[:, 4 * part:4 * part + 4, :],
                              wcr[:, 4 * part:4 * part + 4, :])
            self.s_wc.inc(ld, 16)
        wc_ready = self.s_wc.n

        xres_guard = [[(spe, spe.n)], [(spe, spe.n)]]

        def fin_tile(tt):
            xr = self.xres[tt % 2]
            gcol = (tt % 2) * 8
            s1 = self.s1c[:, gcol:gcol + 8]
            vec = self.vec
            self.dve_edge(dv.scalar_tensor_tensor(vec[:, 5:6], s1[:, 0:1], 1.0, s1[:, 1:2],
                                                  ALU.bypass, ALU.add))
            self.dve_edge(dv.scalar_tensor_tensor(vec[:, 6:7], s1[:, 2:3], 1.0, s1[:, 3:4],
                                                  ALU.bypass, ALU.add))
            self.dve_edge(dv.tensor_tensor(vec[:, 0:1], vec[:, 5:6], vec[:, 6:7], ALU.add))
            self.dve_edge(dv.scalar_tensor_tensor(vec[:, 5:6], s1[:, 4:5], 1.0, s1[:, 5:6],
                                                  ALU.bypass, ALU.add))
            self.dve_edge(dv.scalar_tensor_tensor(vec[:, 6:7], s1[:, 6:7], 1.0, s1[:, 7:8],
                                                  ALU.bypass, ALU.add))
            self.dve_edge(dv.tensor_tensor(vec[:, 1:2], vec[:, 5:6], vec[:, 6:7], ALU.add))
            self.dve_edge(dv.tensor_scalar(vec[:, 0:1], vec[:, 0:1], 1.0 / 2048.0, None,
                                           ALU.mult))
            self.dve_edge(dv.tensor_scalar(vec[:, 1:2], vec[:, 1:2], 1.0 / 2048.0, None,
                                           ALU.mult))
            self.dve_edge(dv.scalar_tensor_tensor(vec[:, 2:3], vec[:, 0:1], 1.0, vec[:, 0:1],
                                                  ALU.bypass, ALU.mult))
            si = dv.tensor_tensor(vec[:, 1:2], vec[:, 1:2], vec[:, 2:3], ALU.subtract)
            sv = sdve.inc(si, 1)
            self.W(ac, sdve, sv)
            a0 = ac.activation(vec[:, 3:4], vec[:, 1:2], AF.Ln, bias=self.biases[:, 0:1],
                               scale=1.0)
            v0 = sact.inc(a0, 1)
            self.W(ac, sact, v0)
            ai = ac.activation(vec[:, 3:4], vec[:, 3:4], AF.Exp, scale=-0.5)
            av = sact.inc(ai, 1)
            self.W(dv, sact, av)
            ni = dv.tensor_scalar(xr, xr, vec[:, 0:1], vec[:, 3:4],
                                  ALU.subtract, ALU.mult)
            nv = sdve.inc(ni, 1)
            if self.debug and b == 0:
                self.W(sy, sdve, nv)
                dst = sy.dma_start(self.d_vec[tt], self.vec)
                self.s_out.inc(dst, 16)
                dst = sy.dma_start(self.d_s1c[tt], self.s1c[:, gcol:gcol + 8])
                sov = self.s_out.inc(dst, 16)
                self.W(dv, self.s_out, sov)
            self.W(g, sdve, nv)
            st1 = g.dma_start(self.xln1[b, tt * 128:(tt + 1) * 128, :], xr)
            st1v = self.s_sc.inc(st1, 16)
            self.W(dv, self.s_sc, self.bfst_guard)
            self.Wg(dv, self.bfst_slab1_guard)
            self.bfst_slab1_guard = None
            ci = dv.tensor_copy(self.bfst, xr)
            cv = sdve.inc(ci, 1)
            self.W(g, sdve, cv)
            st2 = g.dma_start(self.xln1b[b, tt * 128:(tt + 1) * 128, :], self.bfst)
            self.s_sc.inc(st2, 16)
            self.bfst_guard = self.s_sc.n
            xres_guard[tt % 2] = [(self.s_sc, st1v), (sdve, cv)]

        for tt in range(16):
            xr = self.xres[tt % 2]
            gcol = (tt % 2) * 8
            for gd in (xres_guard[tt % 2] or []):
                self.Wg(g, gd)
            ldr = g.dma_start(xr, self.xin[b, tt * 128:(tt + 1) * 128, :])
            sxr = self.s_xr[tt % 2]
            xr_ready = sxr.inc(ldr, 16)
            self.W(dv, sxr, xr_ready)
            self.W(dv, sdve, self.mark_bcbc)
            dv.memset(self.s1c[:, gcol:gcol + 8], 0.0)
            dv.tensor_tensor(xr, xr, self.bc_bc, ALU.add)
            for jr in range(4):
                bk = jr
                self.bank_wait(bk)
                self.W(te, self.s_wc, wc_ready)
                self.W(te, sdve, y_dve)
                for ck in range(16):
                    mm = te.matmul(self.bank[bk], self.qt[:, ck, tt * 128:(tt + 1) * 128],
                                   self.wc_sb[:, ck, jr * 512:(jr + 1) * 512],
                                   start=(ck == 0), stop=(ck == 15))
                pv = spe.inc(mm, 1)
                self.W(dv, spe, pv)
                ev = dv.scalar_tensor_tensor(xr[:, jr * 512:(jr + 1) * 512], self.bank[bk],
                                             1.0, xr[:, jr * 512:(jr + 1) * 512],
                                             ALU.bypass, ALU.add,
                                             accum_out=self.s1c[:, gcol + jr:gcol + jr + 1])
                dvv = sdve.inc(ev, 1)
                self.bank_done(bk, sdve, dvv)
            for hh in range(4):
                dv.scalar_tensor_tensor(self.epen[hh % 2], xr[:, hh * 512:(hh + 1) * 512],
                                        1.0, xr[:, hh * 512:(hh + 1) * 512],
                                        ALU.bypass, ALU.mult,
                                        accum_out=self.s1c[:, gcol + 4 + hh:gcol + 5 + hh])
            if tt >= 1:
                fin_tile(tt - 1)
            self.prep_pump(2)
        fin_tile(15)
        self.slab_guard[0] = (self.s_sc, self.s_sc.n)
        self.slab_guard[1] = (self.s_sc, self.s_sc.n)
        g4 = [(self.s_sc, self.s_sc.n), (self.s_dve, self.s_dve.n)]
        self.xTq_guard[0] = g4
        self.xTq_guard[1] = g4
        self.a4_done = {"pe": spe.n, "dve": sdve.n, "sc": self.s_sc.n, "act": sact.n}
        self.sc_mark[b] = self.s_sc.n

    def emit_F(self):
        nc = self.nc
        te, dv, ac, sy, g = nc.tensor, nc.vector, nc.scalar, nc.sync, nc.gpsimd
        spe, sdve, sact, sld, sgp = self.s_pe, self.s_dve, self.s_act, self.s_ld, self.s_gp

        self.prep_pump(10 ** 9)

        self.W(te, sdve, self.a4_done["dve"])
        for jr in range(4):
            self.W(g, spe, spe.n if jr > 0 else self.a4_done["pe"])
            g.dma_start(self.bcast_row(jr), self.wsdp[:, jr * 512:(jr + 1) * 512])\
                .then_inc(sgp.h, 16)
            sgp.n += 16
            bk = 6 + (jr % 2)
            self.bank_wait(bk)
            self.W(te, sgp, sgp.n)
            mm = te.matmul(self.bank[bk], self.ones_r[0:1, :], self.bcast_row(jr),
                           start=True, stop=True)
            pv = spe.inc(mm, 1)
            self.W(dv, spe, pv)
            cp = dv.tensor_scalar(self.wsdp_bc[:, jr * 512:(jr + 1) * 512], self.bank[bk],
                                  64.0, None, ALU.mult)
            dvv = sdve.inc(cp, 1)
            self.bank_done(bk, sdve, dvv)
        wsdp_ready = sdve.n
        if self.debug:
            self.W(sy, sdve, wsdp_ready)
            dst = sy.dma_start(self.d_wsbc, self.wsdp_bc)
            self.s_out.inc(dst, 16)

        wupr = self.wupT.rearrange("(ck p) i -> p ck i", p=128)
        wgr = self.wgateT.rearrange("(ck p) i -> p ck i", p=128)

        a4pe = self.a4_done["pe"]
        uslab_guard = [(spe, a4pe), (spe, a4pe)]
        gslab_guard = [(spe, a4pe), (spe, a4pe)]
        wd_guard = [(sdve, self.a4_done["dve"])] * 3
        xlres_guard = [None, None]
        hs2_guard = [None, None]
        ups_guard = [None, None]
        xout_guard = [(self.s_sc, self.a4_done["sc"])]
        x2_guard = None
        x3_guard = None
        btmp_first = True

        for c in range(8):
            b = c // 4
            trange = (c % 4) * 512
            if c == 0:
                self.W(sy, self.s_sc, self.sc_mark[b])
                self.W(sy, spe, a4pe)
                for gidx in range(16):
                    ld = sy.dma_start(self.xlnT[:, gidx, :],
                                      self.xln1b[b, trange:trange + 512,
                                                 gidx * 128:(gidx + 1) * 128],
                                      transpose=True)
                    sld.inc(ld, 16)
                self.xlnT_ready = sld.n
            xlnT_ready = self.xlnT_ready

            for gidx in range(16):
                hsb = self.hs2[gidx % 2]
                if c == 0 or gidx >= 2:
                    self.Wg(dv, hs2_guard[gidx % 2])
                    hi = dv.tensor_scalar(hsb, self.h128f, self.suug_c[:, gidx:gidx + 1],
                                          C2, ALU.mult, ALU.mult)
                    hv = sdve.inc(hi, 1)
                else:
                    hv = self.hs2_pre[gidx]
                bk = 4 + (gidx % 2)
                self.bank_wait(bk)
                self.W(te, sdve, hv)
                self.W(te, sld, xlnT_ready)
                if x2_guard is not None:
                    self.Wg(te, x2_guard)
                mm = te.matmul(self.bank[bk], hsb, self.xlnT[:, gidx, :], start=True, stop=True)
                pv = spe.inc(mm, 1)
                hs2_guard[gidx % 2] = (spe, pv)
                self.W(dv, spe, pv)
                ev = dv.tensor_copy(self.x2[:, gidx, :], self.bank[bk])
                dvv = sdve.inc(ev, 1)
                self.bank_done(bk, sdve, dvv)
            if btmp_first:
                self.W(dv, self.s_sc, self.a4_done["sc"])
                btmp_first = False
            kk = 1
            while kk < 16:
                r = 0
                while r < 16:
                    for j in range(kk):
                        a1 = self.x2[:, r + j, :]
                        b1 = self.x2[:, r + j + kk, :]
                        dv.tensor_copy(self.btmp, a1)
                        dv.tensor_tensor(a1, self.btmp, b1, ALU.add)
                        dv.tensor_tensor(b1, self.btmp, b1, ALU.subtract)
                    r += 2 * kk
                kk *= 2
            ii = dv.engine_nop()
            x2_dv = sdve.inc(ii, 1)
            self.f2_pe_done = spe.n

            bf_done = [0] * 64
            bf_prod = [False] * 64
            bf_last = [-10] * 64
            bf_ctr = [0]
            BF_DIST = 6

            def bf_pump(final=False):
                while True:
                    progress = True
                    stuck = False
                    while progress:
                        progress = False
                        for s in range(6):
                            step = 1 << s
                            for i0 in range(0, 64, 2 * step):
                                for j in range(step):
                                    i = i0 + j
                                    jj = i + step
                                    if not (bf_prod[i] and bf_prod[jj]
                                            and bf_done[i] == s and bf_done[jj] == s):
                                        continue
                                    if max(bf_last[i], bf_last[jj]) > bf_ctr[0] - BF_DIST:
                                        stuck = True
                                        continue
                                    a1 = self.x3[:, i, :]
                                    b1 = self.x3[:, jj, :]
                                    dv.tensor_copy(self.btmp, a1)
                                    dv.tensor_tensor(a1, self.btmp, b1, ALU.add)
                                    ii2 = dv.tensor_tensor(b1, self.btmp, b1, ALU.subtract)
                                    bf_ctr[0] += 3
                                    bf_done[i] += 1
                                    bf_done[jj] += 1
                                    bf_last[i] = bf_last[jj] = bf_ctr[0]
                                    progress = True
                                    if final:
                                        bf_last_instr[0] = ii2
                    if final and stuck:
                        self.dve_edge(bf_last_instr[0])
                        for i in range(64):
                            bf_last[i] = -10
                        continue
                    return

            bf_last_instr = [None]

            for ig in range(32):
                us = self.uslab[ig % 2]
                gs = self.gslab[ig % 2]
                sus = self.s_us[ig % 2]
                sgs = self.s_gs[ig % 2]
                self.W(sy, self.s_tp, self.prep_done())
                self.Wg(sy, uslab_guard[ig % 2])
                ld = sy.dma_start(us, wupr[:, :, ig * 256:(ig + 1) * 256])
                sus.inc(ld, 16)
                self.Wg(sy, gslab_guard[ig % 2])
                ld = sy.dma_start(gs, wgr[:, :, ig * 256:(ig + 1) * 256])
                sgs.inc(ld, 16)
                for il in range(2):
                    it = ig * 2 + il
                    ub = il
                    gb = 2 + il
                    self.bank_wait(ub)
                    self.bank_wait(gb)
                    self.W(te, sus, sus.n)
                    self.W(te, sgs, sgs.n)
                    self.W(te, sdve, x2_dv)
                    if x3_guard is not None:
                        self.Wg(te, x3_guard)
                    for ck in range(16):
                        mm = te.matmul(self.bank[ub], us[:, ck, il * 128:(il + 1) * 128],
                                       self.x2[:, ck, :], start=(ck == 0), stop=(ck == 15))
                    upv = spe.inc(mm, 1)
                    for ck in range(16):
                        mm = te.matmul(self.bank[gb], gs[:, ck, il * 128:(il + 1) * 128],
                                       self.x2[:, ck, :], start=(ck == 0), stop=(ck == 15))
                    gpv = spe.inc(mm, 1)
                    if il == 1:
                        uslab_guard[ig % 2] = (spe, upv)
                        gslab_guard[ig % 2] = (spe, gpv)
                    upslot = self.ups[it % 2]
                    self.W(dv, spe, upv)
                    self.Wg(dv, ups_guard[it % 2])
                    ui = dv.tensor_scalar(upslot, self.bank[ub], self.wsu_up[:, it:it + 1],
                                          None, ALU.mult)
                    uv = sdve.inc(ui, 1)
                    self.bank_done(ub, sdve, uv)
                    self.W(ac, spe, gpv)
                    ai = ac.activation(self.x3[:, it, :], self.bank[gb], AF.Silu,
                                       scale=self.wsu_gt[:, it:it + 1])
                    av = sact.inc(ai, 1)
                    self.bank_done(gb, sact, av)
                    self.W(dv, sact, av)
                    xi = dv.scalar_tensor_tensor(self.x3[:, it, :], self.x3[:, it, :],
                                                 self.sudp_c[:, it:it + 1], upslot,
                                                 ALU.mult, ALU.mult)
                    xv = sdve.inc(xi, 1)
                    ups_guard[it % 2] = (sdve, xv)
                    bf_prod[it] = True
                    bf_last[it] = bf_ctr[0]
                    bf_ctr[0] += 1
                    bf_pump()

            bf_last_instr[0] = dv.engine_nop()
            bf_pump(final=True)
            assert all(d == 6 for d in bf_done), bf_done
            ii = dv.engine_nop()
            x3_dv = sdve.inc(ii, 1)
            if self.debug and c == 0:
                self.W(sy, sdve, x2_dv)
                st = sy.dma_start(self.d_x2, self.x2)
                self.s_out.inc(st, 16)
                self.W(sy, sdve, x3_dv)
                st = sy.dma_start(self.d_x3, self.x3)
                self.s_out.inc(st, 16)

            if c + 1 < 8:
                nb2 = (c + 1) // 4
                ntr = ((c + 1) % 4) * 512
                self.W(sy, self.s_sc, self.sc_mark[nb2])
                self.W(sy, spe, self.f2_pe_done)
                for gidx in range(16):
                    ld = sy.dma_start(self.xlnT[:, gidx, :],
                                      self.xln1b[nb2, ntr:ntr + 512,
                                                 gidx * 128:(gidx + 1) * 128],
                                      transpose=True)
                    sld.inc(ld, 16)
                self.xlnT_ready = sld.n

            if self.debug and c == 0:
                self.W(dv, self.s_out, self.s_out.n)
            for itt in range(64):
                bk = 4 + (itt % 2)
                self.bank_wait(bk)
                self.W(te, sdve, x3_dv)
                mm = te.matmul(self.bank[bk], self.h128s, self.x3[:, itt, :],
                               start=True, stop=True)
                pv = spe.inc(mm, 1)
                self.W(dv, spe, pv)
                ev = dv.tensor_copy(self.x3[:, itt, :], self.bank[bk])
                dvv = sdve.inc(ev, 1)
                self.bank_done(bk, sdve, dvv)
            x4_dv = sdve.n

            for jrh in range(2):
                for tau in range(64):
                    ws = self.wdsl[tau % 3]
                    swd = self.s_wd[tau % 3]
                    self.W(sy, self.s_tp, self.prep_done())
                    self.Wg(sy, wd_guard[tau % 3])
                    ld = sy.dma_start(ws[:, 0:1024], self.wdT[tau * 128:(tau + 1) * 128,
                                                               jrh * 1024:(jrh + 1) * 1024])
                    swd.inc(ld, 16)
                    self.W(te, swd, swd.n)
                    self.W(te, sdve, x4_dv)
                    for tt in range(4):
                        for jrl in range(2):
                            if tau == 0:
                                self.bank_wait(tt * 2 + jrl)
                            mm = te.matmul(self.bank[tt * 2 + jrl],
                                           self.x3[:, tau, tt * 128:(tt + 1) * 128],
                                           ws[:, jrl * 512:(jrl + 1) * 512],
                                           start=(tau == 0), stop=(tau == 63))
                    pv = spe.inc(mm, 1)
                    wd_guard[tau % 3] = (spe, pv)
                dn_pe = spe.n
                for tt in range(4):
                    xlr = self.xlres[tt % 2]
                    self.Wg(g, xlres_guard[tt % 2])
                    ldr = g.dma_start(xlr, self.xln1[b, trange + tt * 128:
                                                     trange + (tt + 1) * 128,
                                                     jrh * 1024:(jrh + 1) * 1024])
                    sxl = self.s_xl[tt % 2]
                    rv = sxl.inc(ldr, 16)
                    for jrl in range(2):
                        jcol = jrh * 1024 + jrl * 512
                        bk = tt * 2 + jrl
                        self.W(dv, spe, dn_pe)
                        self.W(dv, sdve, wsdp_ready)
                        for gd in xout_guard:
                            self.Wg(dv, gd)
                        dv.tensor_tensor(self.xout[:, tt, jcol:jcol + 512], self.bank[bk],
                                         self.wsdp_bc[:, jcol:jcol + 512], ALU.mult)
                        self.W(dv, sxl, rv)
                        ei = dv.scalar_tensor_tensor(self.xout[:, tt, jcol:jcol + 512],
                                                     self.xout[:, tt, jcol:jcol + 512], 1.0,
                                                     xlr[:, jrl * 512:(jrl + 1) * 512],
                                                     ALU.bypass, ALU.add)
                        dvv = sdve.inc(ei, 1)
                        self.bank_done(bk, sdve, dvv)
                    xlres_guard[tt % 2] = (sdve, sdve.n)
            xout_guard = []

            vec = self.vec
            for tt in range(4):
                xo = self.xout[:, tt, :]
                gcol = tt * 8
                dv.memset(self.s1c[:, gcol:gcol + 8], 0.0)
                for hh in range(4):
                    dv.scalar_tensor_tensor(self.f6tmp, xo[:, hh * 512:(hh + 1) * 512], 1.0,
                                            xo[:, hh * 512:(hh + 1) * 512],
                                            ALU.bypass, ALU.bypass,
                                            accum_out=self.s1c[:, gcol + hh:gcol + hh + 1])
                    dv.scalar_tensor_tensor(self.f6tmp, xo[:, hh * 512:(hh + 1) * 512],
                                            1.0, xo[:, hh * 512:(hh + 1) * 512],
                                            ALU.bypass, ALU.mult,
                                            accum_out=self.s1c[:, gcol + 4 + hh:gcol + 5 + hh])
            for tt in range(4):
                xo = self.xout[:, tt, :]
                s1 = self.s1c[:, tt * 8:tt * 8 + 8]
                self.dve_edge(dv.scalar_tensor_tensor(vec[:, 5:6], s1[:, 0:1], 1.0,
                                                      s1[:, 1:2], ALU.bypass, ALU.add))
                self.dve_edge(dv.scalar_tensor_tensor(vec[:, 6:7], s1[:, 2:3], 1.0,
                                                      s1[:, 3:4], ALU.bypass, ALU.add))
                self.dve_edge(dv.tensor_tensor(vec[:, 0:1], vec[:, 5:6], vec[:, 6:7],
                                               ALU.add))
                self.dve_edge(dv.scalar_tensor_tensor(vec[:, 5:6], s1[:, 4:5], 1.0,
                                                      s1[:, 5:6], ALU.bypass, ALU.add))
                self.dve_edge(dv.scalar_tensor_tensor(vec[:, 6:7], s1[:, 6:7], 1.0,
                                                      s1[:, 7:8], ALU.bypass, ALU.add))
                self.dve_edge(dv.tensor_tensor(vec[:, 1:2], vec[:, 5:6], vec[:, 6:7],
                                               ALU.add))
                self.dve_edge(dv.tensor_scalar(vec[:, 0:1], vec[:, 0:1], 1.0 / 2048.0,
                                               None, ALU.mult))
                self.dve_edge(dv.tensor_scalar(vec[:, 1:2], vec[:, 1:2], 1.0 / 2048.0,
                                               None, ALU.mult))
                self.dve_edge(dv.scalar_tensor_tensor(vec[:, 2:3], vec[:, 0:1], 1.0,
                                                      vec[:, 0:1], ALU.bypass, ALU.mult))
                si = dv.tensor_tensor(vec[:, 1:2], vec[:, 1:2], vec[:, 2:3], ALU.subtract)
                sv = sdve.inc(si, 1)
                self.W(ac, sdve, sv)
                a0 = ac.activation(vec[:, 3:4], vec[:, 1:2], AF.Ln, bias=self.biases[:, 0:1],
                                   scale=1.0)
                v0 = sact.inc(a0, 1)
                self.W(ac, sact, v0)
                ai = ac.activation(vec[:, 3:4], vec[:, 3:4], AF.Exp, scale=-0.5)
                av = sact.inc(ai, 1)
                self.W(dv, sact, av)
                ni = dv.tensor_scalar(xo, xo, vec[:, 0:1], vec[:, 3:4],
                                      ALU.subtract, ALU.mult)
                nv = sdve.inc(ni, 1)
                self.W(g, sdve, nv)
                st = g.dma_start(self.out[b, trange + tt * 128:trange + (tt + 1) * 128, :], xo)
                self.s_out.inc(st, 16)
            xout_guard = [(self.s_out, self.s_out.n)]
            x2_guard = (spe, spe.n)
            x3_guard = (spe, spe.n)
            if c + 1 < 8:
                self.hs2_pre = {}
                for gidx in range(2):
                    self.Wg(dv, hs2_guard[gidx % 2])
                    hi = dv.tensor_scalar(self.hs2[gidx % 2], self.h128f,
                                          self.suug_c[:, gidx:gidx + 1], C2,
                                          ALU.mult, ALU.mult)
                    self.hs2_pre[gidx] = sdve.inc(hi, 1)

    def build(self):
        self.emit_init()
        for b in range(NB):
            self.emit_A(b)
        self.emit_F()
        self.nc.sync.wait_ge(self.s_out.h, self.s_out.n)
        return self.nc


def _consts():
    h = np.array([[1.0]], dtype=np.float32)
    for _ in range(7):
        h = np.block([[h, h], [h, -h]])
    i = np.arange(128)
    tri = (i[:, None] <= i[None, :]).astype(ml_dtypes.bfloat16)
    eye8 = np.eye(8, dtype=np.float32)
    onesrow = np.ones((1, 128), dtype=np.float32)
    return h.astype(np.float32), tri, eye8, onesrow


def kernel(L0_attn_hidden_states, L0_attn_Wq, L0_attn_Wk, L0_attn_Wv, L0_attn_Wg,
           L0_attn_Wc, L0_attn_bc, L0_ffn_SU_ug, L0_ffn_SU_dp, L0_ffn_Wscale_ug,
           L0_ffn_Wscale_dp, L0_ffn_W_up, L0_ffn_W_gate, L0_ffn_W_down):
    h128, tri, eye8, onesrow = _consts()
    bld = B(debug=False)
    nc = bld.build()
    x = np.ascontiguousarray(np.asarray(L0_attn_hidden_states, np.float32))
    cnp = lambda a: np.ascontiguousarray(np.asarray(a, np.float32))
    shared = dict(
        wq=cnp(L0_attn_Wq), wk=cnp(L0_attn_Wk), wv=cnp(L0_attn_Wv), wgi=cnp(L0_attn_Wg),
        wc=cnp(L0_attn_Wc), bc=cnp(L0_attn_bc).reshape(1, C),
        su_ug=cnp(L0_ffn_SU_ug).reshape(1, C), su_dp=cnp(L0_ffn_SU_dp).reshape(1, INTER),
        wsug=cnp(L0_ffn_Wscale_ug).reshape(1, 2 * INTER),
        wsdp=cnp(L0_ffn_Wscale_dp).reshape(1, C),
        wup=cnp(L0_ffn_W_up), wgate=cnp(L0_ffn_W_gate), wdown=cnp(L0_ffn_W_down),
        h128=h128, tri=tri, eye8=eye8, onesrow=onesrow,
    )
    in_maps = []
    for cidx in range(8):
        mm = dict(shared)
        mm["xin"] = np.ascontiguousarray(x[2 * cidx:2 * cidx + 2])
        in_maps.append(mm)
    res = run_bass_kernel_spmd(nc, in_maps, core_ids=list(range(8)))
    return np.concatenate([np.asarray(r["out"], np.float32) for r in res.results], axis=0)
